# version 19
# baseline (speedup 1.0000x reference)
import numpy as np
import concourse.bass as bass
import concourse.mybir as mybir
from concourse.bass_utils import run_bass_kernel_spmd

F32 = mybir.dt.float32
BF16 = mybir.dt.bfloat16
AF = mybir.ActivationFunctionType
ALU = mybir.AluOpType

B, L, DM = 4, 8192, 1024
DI, DS, DTR, DCONV, DH = 2048, 16, 64, 4, 4096

CH = 1024
NKD = DM // 128
NDT = CH // 128
NMH = DH // 128
TC = 512
NCHUNK = L // TC
EPS = 1e-5
ARENA_ELEMS = 52480


def pack_w(W):
    M, K = W.shape
    mt, kt = max(1, M // 128), max(1, K // 128)
    mm, kk = min(128, M), min(128, K)
    Wr = W.reshape(mt, mm, kt, kk)
    return np.ascontiguousarray(Wr.transpose(0, 3, 2, 1))


def pack_vec(v):
    n = v.shape[0]
    return np.ascontiguousarray(v.reshape(n // 128, 128).T)


class Emit:
    ENGS = ("sync", "tensor", "vector", "scalar", "gpsimd")

    def __init__(self):
        self.q = {e: [] for e in self.ENGS}
        self.sem = {}
        self.rel = {}
        self.default_rel = None

    def phase_barrier(self):
        self.default_rel = [(n, v) for n, v in self.sem.items() if v > 0]

    def inc(self, name, n=1):
        self.sem[name] = self.sem.get(name, 0) + n
        return self.sem[name]

    def val(self, name):
        return self.sem.get(name, 0)

    def on(self, eng):
        def deco(fn):
            self.q[eng].append(fn)
        return deco

    def release(self, key, deps):
        if isinstance(deps, tuple):
            deps = [deps]
        self.rel[key] = list(deps)

    def need(self, key):
        return self.rel.get(key, self.default_rel)


class Arena:
    def __init__(self, tens, n_f32):
        self.t, self.size, self.off = tens, n_f32, 0

    def take(self, shape, dtype=F32):
        n = int(np.prod(shape))
        nf = n if dtype == F32 else (n + 1) // 2
        assert self.off + nf <= self.size, (self.off, nf, self.size)
        ap = self.t[:, self.off:self.off + nf]
        self.off += nf
        if dtype == BF16:
            ap = ap.bitcast(BF16)
        if len(shape) == 2:
            ap = ap.rearrange("p (a b) -> p a b", a=shape[0])
        elif len(shape) == 3:
            ap = ap.rearrange("p (a b c) -> p a b c", a=shape[0], b=shape[1])
        return ap

    def mark(self):
        return self.off

    def reset(self, off=0):
        self.off = off


ENGINE_SEMS = frozenset(["s_pe", "s_ac", "s_dv", "s_gp"])


def _flat_deps(deps, em=None):
    out = {}
    stack = list(deps)
    while stack:
        d = stack.pop()
        if d is None:
            continue
        if isinstance(d, list):
            stack.extend(d)
            continue
        sname, sval = d
        if em is not None and sname not in ENGINE_SEMS:
            sval = em.val(sname)
        if sval > out.get(sname, 0):
            out[sname] = sval
    return tuple(out.items())


def build_program():
    import os
    PH = int(os.environ.get("KPHASES", "4"))
    NCH = int(os.environ.get("KNCHUNK", str(NCHUNK)))
    nc = bass.Bass("TRN2", target_bir_lowering=False, debug=False)

    def din(name, shape):
        return nc.dram_tensor(name, shape, F32, kind="ExternalInput").ap()

    x_fm = din("x_fm", [DM, L])
    x_res = din("x_res", [DM, L // 2])
    w_in = din("w_in", [2 * NDT, 128, NKD, 128])
    w_x = din("w_x", [1, 128, NDT, 96])
    w_dtp = din("w_dtp", [NDT, 64, 1, 128])
    w_out = din("w_out", [NKD, 128, NDT, 128])
    w_1 = din("w_1", [NMH, 128, NKD, 128])
    w_2 = din("w_2", [NMH, 128, NKD, 128])
    w_h1 = din("w_h1", [NMH, 128, NMH, 128])
    w_h2 = din("w_h2", [NMH, 128, NMH, 128])
    w_3 = din("w_3", [NKD, 128, NMH, 128])
    conv_w_p = din("conv_w_p", [128, NDT, DCONV])
    conv_b_p = din("conv_b_p", [128, NDT])
    b_dt_p = din("b_dt_p", [128, NDT])
    d_p = din("d_p", [128, NDT])
    a_log = din("a_log", [1, DS])
    w1s_p = din("w1s_p", [128, NMH])
    w2s_p = din("w2s_p", [128, NMH])
    h1s_p = din("h1s_p", [128, NMH])
    h2s_p = din("h2s_p", [128, NMH])
    w3s_p = din("w3s_p", [128, NKD])

    out_fm = nc.dram_tensor("out_fm", [DM, L // 2], F32,
                            kind="ExternalOutput").ap()

    def scr(name, shape, dt=F32):
        return nc.dram_tensor(name, shape, dt, kind="Internal").ap()

    wb_in = scr("wb_in", [2 * NDT, 128, NKD, 128], BF16)
    wb_x = scr("wb_x", [1, 128, NDT, 96], BF16)
    wb_dtp = scr("wb_dtp", [NDT, 64, 1, 128], BF16)
    wb_out = scr("wb_out", [NKD, 128, NDT, 128], BF16)
    wb_1 = scr("wb_1", [NMH, 128, NKD, 128], BF16)
    wb_2 = scr("wb_2", [NMH, 128, NKD, 128], BF16)
    wb_h1 = scr("wb_h1", [NMH, 128, NMH, 128], BF16)
    wb_h2 = scr("wb_h2", [NMH, 128, NMH, 128], BF16)
    wb_3 = scr("wb_3", [NKD, 128, NMH, 128], BF16)

    xs_dram = scr("xs_dram", [NDT, 128, L], F32)
    z_dram = scr("z_dram", [NDT, 128, L], BF16)
    dt_dram = scr("dt_dram", [NDT, 128, L], F32)
    xdblb_dram = scr("xdblb_dram", [NCHUNK, 32, TC], BF16)
    xag_in = scr("xag_in", [NCHUNK, 96, TC], F32)
    xag_out = scr("xag_out", [NCHUNK, 2, 96, TC], F32)
    rs_in = scr("rs_in", [NCHUNK, 2, NKD, 128, TC // 2], F32)
    rs_out = scr("rs_out", [NCHUNK, NKD, 128, TC // 2], F32)

    em = Emit()

    from contextlib import ExitStack
    with ExitStack() as _es:
        def _e(cm):
            return _es.enter_context(cm)
        S = {}

        def sem(name):
            if name not in S:
                S[name] = _e(nc.semaphore(name))
            return S[name]
        for _n in ("s_pe", "s_ac", "s_dv"):
            sem(_n)
        pA = _e(nc.psum_tensor("pA", [128, 512], F32))
        pB = _e(nc.psum_tensor("pB", [128, 512], F32))
        pS1 = _e(nc.psum_tensor("pS1", [128, 512], F32))
        pS2 = _e(nc.psum_tensor("pS2", [128, 512], F32))
        pC1 = _e(nc.psum_tensor("pC1", [128, 512], F32))
        pC2 = _e(nc.psum_tensor("pC2", [128, 512], F32))
        pC3 = _e(nc.psum_tensor("pC3", [128, 512], F32))
        pC4 = _e(nc.psum_tensor("pC4", [128, 512], F32))
        ones_col = _e(nc.sbuf_tensor("ones_col", [128, 1], BF16))
        ones_row = _e(nc.sbuf_tensor("ones_row", [1, 128], F32))
        ones_bf = _e(nc.sbuf_tensor("ones_bf", [1, 128], BF16))
        vec_convw = _e(nc.sbuf_tensor("vec_convw", [128, NDT, DCONV], F32))
        vec_convb = _e(nc.sbuf_tensor("vec_convb", [128, NDT], F32))
        vec_bdt = _e(nc.sbuf_tensor("vec_bdt", [128, NDT], F32))
        vec_d = _e(nc.sbuf_tensor("vec_d", [128, NDT], F32))
        vw1s = _e(nc.sbuf_tensor("vw1s", [128, NMH], F32))
        vw2s = _e(nc.sbuf_tensor("vw2s", [128, NMH], F32))
        vh1s = _e(nc.sbuf_tensor("vh1s", [128, NMH], F32))
        vh2s = _e(nc.sbuf_tensor("vh2s", [128, NMH], F32))
        vw3s = _e(nc.sbuf_tensor("vw3s", [128, NKD], F32))
        a_row = _e(nc.sbuf_tensor("a_row", [1, DS], F32))
        a_cols = _e(nc.sbuf_tensor("a_cols", [128, DS], F32))
        state = _e(nc.sbuf_tensor("state", [128, NDT, DS], F32))
        halo = _e(nc.sbuf_tensor("halo", [128, NDT, 3], F32))
        eps_col = _e(nc.sbuf_tensor("eps_col", [128, 1], F32))
        arena_t = _e(nc.sbuf_tensor("arena", [128, ARENA_ELEMS], F32))
        block = _e(nc.Block())

        ar = Arena(arena_t, ARENA_ELEMS)
        ESEM = {"tensor": "s_pe", "vector": "s_dv", "scalar": "s_ac",
                "gpsimd": "s_gp"}

        def eop(eng, deps, build, sem_=None, n=1):
            semname = sem_ or ESEM[eng]
            sem(semname)
            for d in _flat_deps(deps, em):
                sem(d[0])
            v = em.inc(semname, n)
            dd = _flat_deps(deps, em)
            for sname, sval in dd:
                assert sval <= em.val(sname), (sname, sval, em.val(sname))

            @em.on(eng)
            def _(e, build=build, dd=dd, semname=semname, n=n):
                for sname, sval in dd:
                    e.wait_ge(S[sname], sval)
                build(e).then_inc(S[semname], n)
            return (semname, v)

        def egroup(eng, deps, build, sem_=None, tag_last=1):
            semname = sem_ or ESEM[eng]
            v = em.inc(semname, tag_last)
            dd = _flat_deps(deps, em)
            for sname, sval in dd:
                assert sval <= em.val(sname), (sname, sval, em.val(sname))

            @em.on(eng)
            def _(e, build=build, dd=dd, semname=semname, tag_last=tag_last):
                for sname, sval in dd:
                    e.wait_ge(S[sname], sval)
                instrs = build(e)
                if not isinstance(instrs, (list, tuple)):
                    instrs = [instrs]
                for ins in instrs[-tag_last:]:
                    ins.then_inc(S[semname], 1)
            return (semname, v)

        def edma(eng, deps, build, sem_, count):
            sem(sem_)
            for d in _flat_deps(deps, em):
                sem(d[0])
            v = em.inc(sem_, 16 * count)
            dd = _flat_deps(deps, em)
            for sname, sval in dd:
                assert sval <= em.val(sname), (sname, sval, em.val(sname))

            @em.on(eng)
            def _(e, build=build, dd=dd, sem_=sem_):
                for sname, sval in dd:
                    e.wait_ge(S[sname], sval)
                for ins in build(e):
                    ins.then_inc(S[sem_], 16)
            return (sem_, v)

        d_setup = edma("sync", [], lambda e: [
            e.dma_start(out=vec_convw[:], in_=conv_w_p),
            e.dma_start(out=vec_convb[:], in_=conv_b_p),
            e.dma_start(out=vec_bdt[:], in_=b_dt_p),
            e.dma_start(out=vec_d[:], in_=d_p),
            e.dma_start(out=vw1s[:], in_=w1s_p),
            e.dma_start(out=vw2s[:], in_=w2s_p),
            e.dma_start(out=vh1s[:], in_=h1s_p),
            e.dma_start(out=vh2s[:], in_=h2s_p),
            e.dma_start(out=vw3s[:], in_=w3s_p),
            e.dma_start(out=a_row[:], in_=a_log)], "s_ld", 10)

        d_init = egroup("vector", [], lambda e: (
            e.memset(ones_col[:], 1.0),
            e.memset(ones_row[:], 1.0),
            e.memset(ones_bf[:], 1.0),
            e.memset(eps_col[:], EPS),
            e.memset(state[:], 0.0),
            e.memset(halo[:], 0.0)), tag_last=5)

        d_ar1 = eop("scalar", [d_setup],
                    lambda e: e.activation(a_row[:], a_row[:], AF.Exp))
        d_ar2 = eop("scalar", [d_ar1],
                    lambda e: e.mul(a_row[:], a_row[:], -1.0))
        d_acm = eop("tensor", [d_ar2, d_init],
                    lambda e: e.matmul(pA[:, 0:DS], ones_row[:], a_row[:],
                                       start=True, stop=True))
        d_acols = eop("scalar", [d_acm],
                      lambda e: e.copy(a_cols[:], pA[:, 0:DS]))
        em.release(("p4", 0), d_acols)

        wc_f = [ar.take([4096]), ar.take([4096])]
        wc_b = [ar.take([4096], BF16), ar.take([4096], BF16)]
        casts = [
            (w_in, wb_in, 2 * NDT, NKD * 128, 128),
            (w_x, wb_x, 1, NDT * 96, 128),
            (w_dtp, wb_dtp, NDT, 128, 64),
            (w_out, wb_out, NKD, NDT * 128, 128),
        ]
        cidx = 0
        for (src, dst, nmt, row, part) in casts:
            for mt in range(nmt):
                bi = cidx % 2
                cidx += 1
                d_l = eop("sync", [em.need(("wc_f", bi))],
                          lambda e, src=src, mt=mt, bi=bi, row=row, part=part:
                          e.dma_start(out=wc_f[bi][0:part, 0:row],
                                      in_=src[mt].rearrange("p a b -> p (a b)")),
                          sem_="s_wcf%d" % bi, n=16)
                d_c = eop("vector", [d_l, em.need(("wc_b", bi))],
                          lambda e, bi=bi, row=row, part=part:
                          e.tensor_copy(wc_b[bi][0:part, 0:row],
                                        wc_f[bi][0:part, 0:row]))
                em.release(("wc_f", bi), d_c)
                d_s = eop("gpsimd", [d_c],
                          lambda e, dst=dst, mt=mt, bi=bi, row=row, part=part:
                          e.dma_start(out=dst[mt].rearrange("p a b -> p (a b)"),
                                      in_=wc_b[bi][0:part, 0:row]),
                          sem_="s_wcb%d" % bi, n=16)
                em.release(("wc_b", bi), d_s)
        m0_done = [("s_wcb0", em.val("s_wcb0")), ("s_wcb1", em.val("s_wcb1"))]
        ar.reset(0)
        em.phase_barrier()

        x_sb = ar.take([NKD, TC])
        tmp_f = [ar.take([TC]), ar.take([TC])]
        xr_t = [ar.take([TC + 4]), ar.take([TC + 4])]
        conv_t = ar.take([TC])
        cv4 = ar.take([TC])
        xs_f = ar.take([NDT, TC])
        xdbl = ar.take([TC])
        xd2 = ar.take([2, TC])
        xdbl_f = ar.take([TC])
        sp_na = ar.take([TC])
        sp_e = ar.take([TC])
        sp_m = [ar.take([TC]), ar.take([TC])]
        sp_l = [ar.take([TC]), ar.take([TC])]
        dt_t = [ar.take([TC]), ar.take([TC]), ar.take([TC]), ar.take([TC])]
        rows = ar.take([4, TC])
        xbf = [ar.take([TC], BF16), ar.take([TC], BF16)]
        xsq = [ar.take([TC], BF16), ar.take([TC], BF16)]
        ln1 = ar.take([NKD, TC], BF16)
        xs_bf = ar.take([NDT, TC], BF16)
        z_t = [ar.take([TC], BF16), ar.take([TC], BF16),
               ar.take([TC], BF16), ar.take([TC], BF16)]
        xdbl_bf = ar.take([TC], BF16)
        wt_in = ar.take([2 * NDT * NKD * 128], BF16)
        wt_x = ar.take([NDT * 96], BF16)
        wt_dt = ar.take([NDT * 128], BF16)
        wc2_f = [ar.take([4096]), ar.take([4096])]
        wc2_b = [ar.take([4096], BF16), ar.take([4096], BF16)]
        assert ar.mark() <= ARENA_ELEMS, ar.mark()

        def ffn_cast_stream():
            fcasts = [
                (w_1, wb_1, NMH, NKD * 128, 128),
                (w_2, wb_2, NMH, NKD * 128, 128),
                (w_h1, wb_h1, NMH, DH, 128),
                (w_h2, wb_h2, NMH, DH, 128),
                (w_3, wb_3, NKD, DH, 128),
            ]
            ci2 = 0
            for (srct, dst, nmt, row, part) in fcasts:
                for mt in range(nmt):
                    bi = ci2 % 2
                    qin, qout = ("sync", "gpsimd") if bi == 0 else (
                        "gpsimd", "sync")
                    ci2 += 1
                    d_l = eop(qin, [em.need(("wc2_f", bi))],
                              lambda e, srct=srct, mt=mt, bi=bi, row=row:
                              e.dma_start(out=wc2_f[bi][0:128, 0:row],
                                          in_=srct[mt].rearrange(
                                              "p a b -> p (a b)")),
                              sem_="s_wc2f%d" % bi, n=16)
                    d_c = eop("scalar", [d_l, em.need(("wc2_b", bi))],
                              lambda e, bi=bi, row=row:
                              e.copy(wc2_b[bi][0:128, 0:row],
                                     wc2_f[bi][0:128, 0:row]))
                    em.release(("wc2_f", bi), d_c)
                    d_s = eop(qout, [d_c],
                              lambda e, dst=dst, mt=mt, bi=bi, row=row:
                              e.dma_start(out=dst[mt].rearrange(
                                              "p a b -> p (a b)"),
                                          in_=wc2_b[bi][0:128, 0:row]),
                              sem_="s_wc2b%d" % bi, n=16)
                    em.release(("wc2_b", bi), d_s)
                    yield "cast"

        def emit_ln(x_view, x_dep, ln_out, xbf_, xsq_, rows_, tmp_, kp,
                    xto_key=None):
            nb, nq, nt = len(xbf_), len(xsq_), len(tmp_)
            tkey = (lambda k: (xto_key, k % nt)) if xto_key else (
                lambda k: (kp + "tmp", k % nt))
            d_mm = None
            for k in range(NKD):
                xd = x_dep[k] if isinstance(x_dep, list) else x_dep
                d_cb = eop("vector", [xd, em.need((kp + "xbf", k % nb)), d_init],
                           lambda e, k=k: e.tensor_copy(xbf_[k % nb][:],
                                                        x_view[:, k, :]))
                d_sq = eop("scalar", [xd, em.need((kp + "xsq", k % nq))],
                           lambda e, k=k: e.activation(xsq_[k % nq][:],
                                                       x_view[:, k, :],
                                                       AF.Square))
                d_mm = egroup("tensor",
                              [d_cb, d_sq,
                               em.need(("p4", 2)) if k == 0 else None,
                               em.need(("p4", 3)) if k == 0 else None],
                              lambda e, k=k: (
                                  e.matmul(pS1[0:1, 0:TC], ones_col[:],
                                           xbf_[k % nb][:], start=(k == 0),
                                           stop=(k == NKD - 1)),
                                  e.matmul(pS2[0:1, 0:TC], ones_col[:],
                                           xsq_[k % nq][:], start=(k == 0),
                                           stop=(k == NKD - 1))),
                              tag_last=2)
                em.release((kp + "xbf", k % nb), d_mm)
                em.release((kp + "xsq", k % nq), d_mm)

            rw = em.need(kp + "rows")
            d_mean = eop("vector", [d_mm, rw],
                         lambda e: e.tensor_scalar_mul(rows_[0:1, 0, :],
                                                       pS1[0:1, 0:TC], 1.0 / DM))
            d_msq = eop("vector", [d_mm, rw],
                        lambda e: e.tensor_scalar_mul(rows_[0:1, 1, :],
                                                      pS2[0:1, 0:TC], 1.0 / DM))
            d_m2 = eop("vector", [d_mean],
                       lambda e: e.tensor_mul(rows_[0:1, 2, :], rows_[0:1, 0, :],
                                              rows_[0:1, 0, :]))
            d_var = eop("vector", [d_msq, d_m2],
                        lambda e: e.tensor_sub(rows_[0:1, 1, :],
                                               rows_[0:1, 1, :],
                                               rows_[0:1, 2, :]))
            d_ln = eop("scalar", [d_var],
                       lambda e: e.activation(rows_[0:1, 2, :], rows_[0:1, 1, :],
                                              AF.Ln, bias=eps_col[0:1, :]))
            d_rstd = eop("scalar", [d_ln],
                         lambda e: e.activation(rows_[0:1, 1, :],
                                                rows_[0:1, 2, :],
                                                AF.Exp, scale=-0.5))
            d_B = eop("vector", [d_rstd],
                      lambda e: e.tensor_mul(rows_[0:1, 2, :], rows_[0:1, 0, :],
                                             rows_[0:1, 1, :]))
            d_bc = egroup("tensor", [d_B],
                          lambda e: (
                              e.matmul(pS1[:, 0:TC], ones_row[:],
                                       rows_[0:1, 1, :], start=True, stop=True),
                              e.matmul(pS2[:, 0:TC], ones_row[:],
                                       rows_[0:1, 2, :], start=True,
                                       stop=True)), tag_last=2)
            em.release(kp + "rows", d_bc)
            norm_deps = []
            for k in range(NKD):
                xd = x_dep[k] if isinstance(x_dep, list) else x_dep
                d_t = eop("vector", [d_bc, xd, em.need(tkey(k))],
                          lambda e, k=k: e.tensor_mul(tmp_[k % nt][:],
                                                      x_view[:, k, :],
                                                      pS1[:, 0:TC]))
                d_n = eop("vector", [d_t, em.need((kp + "ln", k))],
                          lambda e, k=k: e.tensor_sub(ln_out[:, k, :],
                                                      tmp_[k % nt][:],
                                                      pS2[:, 0:TC]))
                em.release(tkey(k), d_n)
                norm_deps.append(d_n)
            em.release(("p4", 2), norm_deps[-1])
            em.release(("p4", 3), norm_deps[-1])
            return norm_deps

        def m1_stream():
          d_wi = edma("sync", [m0_done],
                      lambda e: [e.dma_start(
                          out=wt_in[:, m * NKD * 128:(m + 1) * NKD * 128],
                          in_=wb_in[m].rearrange("p a b -> p (a b)"))
                          for m in range(2 * NDT)],
                      "s_wi", 2 * NDT)
          d_wx = eop("sync", [m0_done],
                     lambda e: e.dma_start(
                         out=wt_x[:],
                         in_=wb_x[0].rearrange("p a b -> p (a b)")),
                     sem_="s_wx", n=16)
          d_wdt = eop("sync", [m0_done],
                      lambda e: e.dma_start(
                          out=wt_dt[0:64, :].rearrange("p (a b) -> p a b",
                                                       a=NDT),
                          in_=wb_dtp.rearrange("m k o c -> k m (o c)")),
                      sem_="s_wdt2", n=16)

          def part_a(ci):
            d_x = eop("sync", [em.need("x_sb")],
                      lambda e, ci=ci: e.dma_start(
                          out=x_sb[:],
                          in_=x_fm[:, ci * TC:(ci + 1) * TC].rearrange(
                              "(k p) t -> p k t", p=128)),
                      sem_="s_x", n=16)

            ln1_deps = emit_ln(x_sb, d_x, ln1, xbf, xsq, rows, tmp_f, "m1")
            em.release("x_sb", ln1_deps[-1])
            d_ln1 = ln1_deps[-1]
            yield "m1"

            xs_cast_deps = []
            d_xz_last = None
            xz_order = [t for pair in zip(range(NDT), range(NDT, 2 * NDT))
                        for t in pair]
            for pos, m in enumerate(xz_order):
                bk = pos % 2
                d_mm = egroup("tensor",
                              [d_wi if ci == 0 and pos == 0 else None,
                               d_ln1 if pos == 0 else None,
                               em.need(("pAB", bk))],
                              lambda e, m=m, bk=bk: [
                                  e.matmul((pA if bk == 0 else pB)[:, 0:TC],
                                           wt_in[:, (m * NKD + k) * 128:
                                                 (m * NKD + k + 1) * 128],
                                           ln1[:, k, :], start=(k == 0),
                                           stop=(k == NKD - 1))
                                  for k in range(NKD)])
                d_xz_last = d_mm

                if m < NDT:
                    d_ev = eop("scalar", [d_mm, em.need(("xr_t", m % 2))],
                               lambda e, m=m, bk=bk: e.copy(
                                   xr_t[m % 2][:, 3:3 + TC],
                                   (pA if bk == 0 else pB)[:, 0:TC]))
                    em.release(("pAB", bk), d_ev)

                    d_hi = eop("vector", [d_ev, em.need(("halo", m)), d_init],
                               lambda e, m=m: e.tensor_copy(
                                   xr_t[m % 2][:, 0:3], halo[:, m, :]))
                    d_c4 = eop("vector", [d_ev, em.need("cv4")],
                               lambda e, m=m: e.tensor_scalar_mul(
                                   cv4[:], xr_t[m % 2][:, 3:3 + TC],
                                   vec_convw[:, m, 3:4]))
                    d_cc = d_c4
                    for kk in range(3):
                        d_cc = eop("vector",
                                   [d_cc, d_hi,
                                    em.need("conv_t") if kk == 0 else None],
                                   lambda e, m=m, kk=kk: (
                                       e.scalar_tensor_tensor(
                                           conv_t[:],
                                           xr_t[m % 2][:, kk:kk + TC],
                                           vec_convw[:, m, kk:kk + 1],
                                           cv4[:] if kk == 0 else conv_t[:],
                                           ALU.mult, ALU.add)))
                    em.release("cv4", d_cc)
                    d_ho = eop("vector", [d_hi, d_ev],
                               lambda e, m=m: e.tensor_copy(
                                   halo[:, m, :], xr_t[m % 2][:, TC:TC + 3]))
                    em.release(("halo", m), d_ho)
                    em.release(("xr_t", m % 2), [d_cc, d_ho])

                    d_si = eop("scalar", [d_cc, em.need(("xs_f", m))],
                               lambda e, m=m: e.activation(
                                   xs_f[:, m, :], conv_t[:], AF.Silu,
                                   bias=vec_convb[:, m:m + 1]))
                    em.release("conv_t", d_si)

                    d_cb = eop("vector", [d_si, em.need(("xs_bf", m))],
                               lambda e, m=m: e.tensor_copy(
                                   xs_bf[:, m, :], xs_f[:, m, :]))
                    xs_cast_deps.append(d_cb)

                    d_sp = eop("gpsimd", [d_si],
                               lambda e, m=m, ci=ci: e.dma_start(
                                   out=xs_dram[m, :, ci * TC:(ci + 1) * TC],
                                   in_=xs_f[:, m, :]),
                               sem_="s_xs%d" % m, n=16)
                    em.release(("xs_f", m), [d_cb, d_sp])
                else:
                    zi = m - NDT
                    zb = zi % 4
                    d_ev = eop("scalar", [d_mm, em.need(("z_t", zb))],
                               lambda e, zb=zb, bk=bk: e.activation(
                                   z_t[zb][:],
                                   (pA if bk == 0 else pB)[:, 0:TC],
                                   AF.Identity))
                    em.release(("pAB", bk), d_ev)
                    d_sp = eop("gpsimd", [d_ev],
                               lambda e, zi=zi, zb=zb, ci=ci: e.dma_start(
                                   out=z_dram[zi, :, ci * TC:(ci + 1) * TC],
                                   in_=z_t[zb][:]),
                               sem_="s_z%d" % zb, n=16)
                    em.release(("z_t", zb), d_sp)
                if pos % 4 == 3:
                    yield "m1"
            for k in range(NKD):
                em.release(("m1ln", k), d_xz_last)

            d_xd = egroup("tensor",
                          [d_wx if ci == 0 else None,
                           em.need("pC3")] + xs_cast_deps,
                          lambda e: [
                              e.matmul(pC3[0:96, 0:TC],
                                       wt_x[:, k * 96:(k + 1) * 96],
                                       xs_bf[:, k, :], start=(k == 0),
                                       stop=(k == NDT - 1))
                              for k in range(NDT)])
            for m in range(NDT):
                em.release(("xs_bf", m), d_xd)

            d_xde = eop("scalar", [d_xd, em.need("xdbl")],
                        lambda e: e.copy(xdbl[0:96, :], pC3[0:96, 0:TC]))
            em.release("pC3", d_xde)
            d_xsp = eop("gpsimd", [d_xde],
                        lambda e, ci=ci: e.dma_start(
                            out=xag_in[ci], in_=xdbl[0:96, :]),
                        sem_="s_xd", n=16)
            em.release("xdbl", d_xsp)
            d_ag = eop("gpsimd", [d_xsp],
                       lambda e, ci=ci: e.collective_compute(
                           "AllGather", ALU.bypass,
                           ins=[xag_in[ci]], outs=[xag_out[ci]],
                           replica_groups=[[0, 1], [2, 3], [4, 5], [6, 7]]),
                       sem_="s_ag", n=1)
            ag_dep[ci] = d_ag
            yield "m1"

          def part_b(ci):
            d_xl = eop("sync", [ag_dep[ci], em.need("xd2")],
                       lambda e, ci=ci: e.dma_start(
                           out=xd2[0:96, :, :],
                           in_=xag_out[ci].rearrange("r p t -> p r t")),
                       sem_="s_xl", n=16)
            d_xsum = eop("vector", [d_xl, em.need("xdbl_f")],
                         lambda e: e.tensor_add(xdbl_f[0:96, :],
                                                xd2[0:96, 0, :],
                                                xd2[0:96, 1, :]))
            em.release("xd2", d_xsum)
            d_xbf = eop("vector", [d_xsum, em.need("xdbl_bf")],
                        lambda e: e.tensor_copy(xdbl_bf[0:96, :],
                                                xdbl_f[0:96, :]))
            em.release("xdbl_f", d_xbf)
            d_xfsp = eop("gpsimd", [d_xbf],
                         lambda e, ci=ci: e.dma_start(
                             out=xdblb_dram[ci, :, :],
                             in_=xdbl_bf[64:96, :]),
                         sem_="s_xd2", n=16)
            yield "m1"

            d_dtmm = None
            for m in range(NDT):
                d_dtmm = eop("tensor",
                             [d_wdt if ci == 0 and m == 0 else None,
                              d_xbf if m == 0 else None,
                              em.need(("pDT", m % 2))],
                             lambda e, m=m: e.matmul(
                                 (pC1 if m % 2 == 0 else pC2)[:, 0:TC],
                                 wt_dt[0:64, m * 128:(m + 1) * 128],
                                 xdbl_bf[0:64, :], start=True, stop=True))

                d_abs = eop("scalar", [d_dtmm, em.need("sp_na")],
                            lambda e, m=m: e.activation(
                                sp_na[:],
                                (pC1 if m % 2 == 0 else pC2)[:, 0:TC],
                                AF.Abs, bias=vec_bdt[:, m:m + 1]))
                d_rel = eop("scalar", [d_dtmm, em.need(("sp_m", m % 2))],
                            lambda e, m=m: e.activation(
                                sp_m[m % 2][:],
                                (pC1 if m % 2 == 0 else pC2)[:, 0:TC],
                                AF.Relu, bias=vec_bdt[:, m:m + 1]))
                em.release(("pDT", m % 2), d_rel)
                d_exp = eop("scalar", [d_abs, em.need("sp_e")],
                            lambda e: e.activation(sp_e[:], sp_na[:], AF.Exp,
                                                   scale=-1.0))
                em.release("sp_na", d_exp)
                d_lg = eop("scalar", [d_exp, em.need(("sp_l", m % 2))],
                           lambda e, m=m: e.activation(sp_l[m % 2][:], sp_e[:],
                                                       AF.Ln, bias=1.0))
                em.release("sp_e", d_lg)

                d_dt = eop("vector", [d_rel, d_lg, em.need(("dt_t", m % 4))],
                           lambda e, m=m: e.tensor_add(dt_t[m % 4][:],
                                                       sp_m[m % 2][:],
                                                       sp_l[m % 2][:]))
                em.release(("sp_m", m % 2), d_dt)
                em.release(("sp_l", m % 2), d_dt)
                d_spd = eop("gpsimd", [d_dt],
                            lambda e, m=m, ci=ci: e.dma_start(
                                out=dt_dram[m, :, ci * TC:(ci + 1) * TC],
                                in_=dt_t[m % 4][:]),
                            sem_="s_dts%d" % (m % 4), n=16)
                em.release(("dt_t", m % 4), d_spd)
                if m % 4 == 3:
                    yield "m1"
            em.release("xdbl_bf", [d_dtmm, d_xfsp])

          ag_dep = {}
          for ci in range(NCH if PH >= 2 else 0):
            yield from part_a(ci)
            if ci > 0:
                yield from part_b(ci - 1)
          if PH >= 2:
            yield from part_b(NCH - 1)

        g_m1 = m1_stream()
        g_cast = ffn_cast_stream()
        for _ in g_m1:
            next(g_cast, None)
            next(g_cast, None)
        for _ in g_cast:
            pass
        ffn_cast_done = [("s_wc2b0", em.val("s_wc2b0")),
                         ("s_wc2b1", em.val("s_wc2b1"))]

        m1_spill_sems = (["s_xs%d" % m for m in range(NDT)] +
                         ["s_z%d" % i for i in range(4)] +
                         ["s_xd", "s_xd2"] +
                         ["s_dts%d" % i for i in range(4)])
        m1_sp_done = [(n, em.val(n)) for n in m1_spill_sems if em.val(n) > 0]
        ar.reset(0)
        em.phase_barrier()
        m2_dt = ar.take([NDT, TC])
        m2_dtx = ar.take([NDT, TC])
        y_acc = ar.take([NDT, TC])
        abar = [ar.take([TC]), ar.take([TC])]
        u_t = [ar.take([TC]), ar.take([TC])]
        h_t = [ar.take([TC]), ar.take([TC])]
        p_t = [ar.take([TC])]
        mixp = [ar.take([TC])]
        xs_g = [ar.take([TC])]
        bc_row = ar.take([2, TC], BF16).rearrange(
            "p a t -> p (a t)").rearrange(
            "p (o a t) -> p o a t", o=1, a=2)
        z_g = [ar.take([TC], BF16)]
        g_all = ar.take([NDT, TC], BF16)
        wt_o = [ar.take([NDT * 128], BF16)]
        x2 = ar.take([NKD, TC])
        xto = [ar.take([TC]), ar.take([TC])]
        u1t = [ar.take([TC])]
        u2t = [ar.take([TC])]
        rows2 = ar.take([3, TC])
        ln2 = ar.take([NKD, TC], BF16)
        hbuf = ar.take([NMH, TC], BF16)
        h3buf = ar.take([NMH, TC], BF16)
        wt_h = [ar.take([DH], BF16), ar.take([DH], BF16)]
        wt_s = [ar.take([NKD * 128], BF16), ar.take([NKD * 128], BF16)]
        xbf2 = [ar.take([TC], BF16)]
        xsq2 = [ar.take([TC], BF16)]
        assert ar.mark() <= ARENA_ELEMS, ar.mark()

        y_dep = {}
        st_dep = {}
        rs_deps = []
        m2_pre = {}

        def emit_m2_loads(ci, first):
            HD = NDT // 2
            d_a = edma("sync",
                       [m1_sp_done if first else None,
                        em.need(("m2_dt", 0)), em.need(("m2_dtx", 0))],
                       lambda e, ci=ci: [
                           e.dma_start(out=m2_dt[:, 0:HD, :],
                                       in_=dt_dram[0:HD, :,
                                                   ci * TC:(ci + 1) * TC]
                                       .rearrange("m p t -> p m t")),
                           e.dma_start(out=m2_dtx[:, 0:HD, :],
                                       in_=xs_dram[0:HD, :,
                                                   ci * TC:(ci + 1) * TC]
                                       .rearrange("m p t -> p m t"))],
                       "s_m2", 2)
            d_b = edma("sync",
                       [m1_sp_done if first else None,
                        em.need(("m2_dt", 1)), em.need(("m2_dtx", 1))],
                       lambda e, ci=ci: [
                           e.dma_start(out=m2_dt[:, HD:NDT, :],
                                       in_=dt_dram[HD:NDT, :,
                                                   ci * TC:(ci + 1) * TC]
                                       .rearrange("m p t -> p m t")),
                           e.dma_start(out=m2_dtx[:, HD:NDT, :],
                                       in_=xs_dram[HD:NDT, :,
                                                   ci * TC:(ci + 1) * TC]
                                       .rearrange("m p t -> p m t"))],
                       "s_m2", 2)
            return d_a, d_b

        def m2_chunk_stream(ci, first):
            g_deps = []
            HD = NDT // 2
            if ci in m2_pre:
                d_m2a, d_m2b = m2_pre.pop(ci)
            else:
                d_m2a, d_m2b = emit_m2_loads(ci, first)

            dtx_dep = {}
            for m in range(NDT):
                dtx_dep[m] = eop("vector",
                                 [d_m2a if m < HD else d_m2b],
                                 lambda e, m=m: e.tensor_mul(
                                     m2_dtx[:, m, :], m2_dt[:, m, :],
                                     m2_dtx[:, m, :]))

            def d_m2_for(m):
                return d_m2a if m < HD else d_m2b

            bc_dep = {}

            def emit_bc(s):
                sb = s % 2
                d_bcl = edma("sync", [em.need("bc_row")],
                             lambda e, ci=ci, s=s: [
                                 e.dma_start(out=bc_row[0:1, 0, 0, :],
                                             in_=xdblb_dram[ci,
                                                            s:s + 1, :]),
                                 e.dma_start(out=bc_row[0:1, 0, 1, :],
                                             in_=xdblb_dram[ci,
                                                            16 + s:17 + s, :])],
                             "s_bcl", 2)
                d_bc = egroup("tensor", [d_bcl, em.need(("pBC", sb))],
                              lambda e, s=s, sb=sb: (
                                  e.matmul((pC1 if sb == 0 else pC3)[:, 0:TC],
                                           ones_bf[:], bc_row[0:1, 0, 0, :],
                                           start=True, stop=True),
                                  e.matmul((pC2 if sb == 0 else pC4)[:, 0:TC],
                                           ones_bf[:], bc_row[0:1, 0, 1, :],
                                           start=True, stop=True)),
                              tag_last=2)
                em.release("bc_row", d_bc)
                bc_dep[s] = d_bc

            emit_bc(0)
            for s in range(DS):
                sb = s % 2
                if s + 1 < DS:
                    emit_bc(s + 1)
                d_bc = bc_dep[s]
                d_ab = {}
                d_u = {}
                d_sc = {}
                d_ym = {}
                last_dv = None
                for mm in range(NDT + 2):
                    if mm < NDT:
                        m, ab = mm, (s * NDT + mm) % 2
                        d_ab[m] = eop(
                            "scalar",
                            [d_m2_for(m) if s == 0 else None,
                             d_acols if (ci == 0 and s == 0 and m == 0)
                             else None,
                             em.need(("abar", ab))],
                            lambda e, s=s, m=m, ab=ab: e.activation(
                                abar[ab][:], m2_dt[:, m, :], AF.Exp,
                                scale=a_cols[:, s:s + 1]))
                    if 1 <= mm <= NDT:
                        m, ub, ab = mm - 1, (mm - 1) % 2, (s * NDT + mm - 1) % 2
                        d_sc[m] = eop(
                            "vector",
                            [d_ab[m], d_u[m], st_dep.get((m, s)),
                             em.need(("h_t", ub))],
                            lambda e, s=s, m=m, ub=ub, ab=ab:
                            e.tensor_tensor_scan(
                                h_t[ub][:], abar[ab][:], u_t[ub][:],
                                state[:, m, s:s + 1], ALU.mult, ALU.add))
                        em.release(("abar", ab), d_sc[m])
                        em.release(("u_t", ub), d_sc[m])
                    if mm < NDT:
                        m, ub = mm, mm % 2
                        d_u[m] = eop(
                            "vector",
                            [d_bc, dtx_dep[m] if s == 0 else None,
                             em.need(("u_t", ub))],
                            lambda e, s=s, m=m, ub=ub: e.tensor_mul(
                                u_t[ub][:], m2_dtx[:, m, :],
                                (pC1 if s % 2 == 0 else pC3)[:, 0:TC]))
                    if mm >= 2:
                        m, ub = mm - 2, (mm - 2) % 2
                        if s == 0:
                            d_y = eop("vector", [d_sc[m], y_dep.get(m)],
                                      lambda e, s=s, m=m, ub=ub: e.tensor_mul(
                                          y_acc[:, m, :], h_t[ub][:],
                                          (pC2 if s % 2 == 0
                                           else pC4)[:, 0:TC]))
                            d_st = eop("vector", [d_sc[m]],
                                       lambda e, s=s, m=m, ub=ub:
                                       e.tensor_copy(state[:, m, s:s + 1],
                                                     h_t[ub][:, TC - 1:TC]))
                            em.release(("h_t", ub), [d_y, d_st])
                        else:
                            d_ym[m] = eop("vector",
                                          [d_sc[m], em.need("p_t")],
                                          lambda e, s=s, m=m, ub=ub:
                                          e.tensor_mul(
                                              p_t[0][:], h_t[ub][:],
                                              (pC2 if s % 2 == 0
                                               else pC4)[:, 0:TC]))
                            d_st = eop("vector", [d_sc[m]],
                                       lambda e, s=s, m=m, ub=ub:
                                       e.tensor_copy(state[:, m, s:s + 1],
                                                     h_t[ub][:, TC - 1:TC]))
                            em.release(("h_t", ub), [d_ym[m], d_st])
                            d_y = eop("vector", [d_ym[m], y_dep.get(m)],
                                      lambda e, m=m: e.tensor_add(
                                          y_acc[:, m, :], y_acc[:, m, :],
                                          p_t[0][:]))
                            em.release("p_t", d_y)
                        st_dep[(m, s)] = d_st
                        y_dep[m] = d_y
                        last_dv = d_y
                    if mm % 3 == 2:
                        yield "s"
                em.release(("pBC", sb), last_dv)
            em.release(("m2_dt", 0), d_ab[HD - 1])
            em.release(("m2_dt", 1), d_ab[NDT - 1])
            em.release(("m2_dtx", 0), d_u[HD - 1])
            em.release(("m2_dtx", 1), d_u[NDT - 1])
            if ci + 1 < NCH:
                m2_pre[ci + 1] = emit_m2_loads(ci + 1, False)

            for m in range(NDT):
                d_gl = edma("sync",
                            [em.need("xs_g"), em.need("z_g")],
                            lambda e, ci=ci, m=m: [
                                e.dma_start(out=xs_g[0][:],
                                            in_=xs_dram[m, :,
                                                        ci * TC:(ci + 1) * TC]),
                                e.dma_start(out=z_g[0][:],
                                            in_=z_dram[m, :,
                                                       ci * TC:(ci + 1) * TC])],
                            "s_gt", 2)
                d_sz = eop("scalar", [d_gl, em.need("mixp")],
                           lambda e, m=m: e.activation(
                               mixp[0][:], z_g[0][:], AF.Silu))
                em.release("z_g", d_sz)
                d_t1 = eop("vector",
                           [d_gl, y_dep[m], em.need("xs_g")],
                           lambda e, m=m: e.scalar_tensor_tensor(
                               xs_g[0][:], xs_g[0][:], vec_d[:, m:m + 1],
                               y_acc[:, m, :], ALU.mult, ALU.add))
                y_dep[m] = d_t1
                d_g = eop("vector", [d_t1, d_sz, em.need(("g_all", m))],
                          lambda e, m=m: e.tensor_mul(
                              g_all[:, m, :], xs_g[0][:], mixp[0][:]))
                em.release("mixp", d_g)
                em.release("xs_g", d_g)
                g_deps.append(d_g)
                if m % 4 == 3:
                    yield "gate"

            d_mo = None
            pend = None

            def wout_evac(mt, d_mm_):
                d_me = eop("scalar", [d_mm_, em.need("mixp")],
                           lambda e, mt=mt: e.copy(
                               mixp[0][:],
                               (pC3 if mt % 2 == 0 else pC4)[:, 0:TC]))
                em.release(("pwo", mt % 2), d_me)
                wout_evac.last = d_me
                d_ms = edma("gpsimd", [d_me],
                            lambda e, ci=ci, mt=mt: [
                                e.dma_start(
                                    out=rs_in[ci, 0, mt].rearrange(
                                        "p (g w) -> p g w", w=64),
                                    in_=mixp[0].rearrange(
                                        "p (g two w) -> p g two w",
                                        two=2, w=64)[:, :, 0, :]),
                                e.dma_start(
                                    out=rs_in[ci, 1, mt].rearrange(
                                        "p (g w) -> p g w", w=64),
                                    in_=mixp[0].rearrange(
                                        "p (g two w) -> p g two w",
                                        two=2, w=64)[:, :, 1, :])],
                            "s_mx", 2)
                em.release("mixp", d_ms)

            for mt in range(NKD):
                d_wo = eop("sync", [em.need("wt_o"), m0_done],
                           lambda e, mt=mt: e.dma_start(
                               out=wt_o[0][:],
                               in_=wb_out[mt].rearrange("p a b -> p (a b)")),
                           sem_="s_wo", n=16)
                d_mo = egroup("tensor",
                              [d_wo, em.need(("pwo", mt % 2)),
                               em.need(("pBC", 1)) if mt < 2 else None] +
                              (g_deps if mt == 0 else []),
                              lambda e, mt=mt: [
                                  e.matmul(
                                      (pC3 if mt % 2 == 0 else pC4)[:, 0:TC],
                                      wt_o[0][:, k * 128:(k + 1) * 128],
                                      g_all[:, k, :], start=(k == 0),
                                      stop=(k == NDT - 1))
                                  for k in range(NDT)])
                em.release("wt_o", d_mo)
                if pend is not None:
                    wout_evac(*pend)
                pend = (mt, d_mo)
                if mt % 4 == 3:
                    yield "wout"
            wout_evac(*pend)
            em.release(("pBC", 1), wout_evac.last)
            for m in range(NDT):
                em.release(("g_all", m), d_mo)
            d_rs = eop("gpsimd", [("s_mx", em.val("s_mx"))],
                       lambda e, ci=ci: e.collective_compute(
                           "ReduceScatter", ALU.add,
                           ins=[rs_in[ci]], outs=[rs_out[ci]],
                           replica_groups=[[0, 1], [2, 3], [4, 5], [6, 7]]),
                       sem_="s_rs", n=1)
            rs_deps.append(d_rs)
            yield "rs"

        def ffn_block_stream(fb):
            fsl = slice(fb * TC, (fb + 1) * TC)
            d_mg = edma("sync",
                        [rs_deps[2 * fb + 1], em.need("x2w")],
                        lambda e, fb=fb: [
                            e.dma_start(out=x2[:, :, 0:TC // 2],
                                        in_=rs_out[2 * fb].rearrange(
                                            "m p t -> p m t")),
                            e.dma_start(out=x2[:, :, TC // 2:TC],
                                        in_=rs_out[2 * fb + 1].rearrange(
                                            "m p t -> p m t"))],
                        "s_ff", 2)

            x2_deps = []
            for m in range(NKD):
                xb = m % 2
                d_xr = eop("sync", [em.need(("xto", xb))],
                           lambda e, m=m, fsl=fsl, xb=xb: e.dma_start(
                               out=xto[xb][:],
                               in_=x_res[m * 128:(m + 1) * 128, fsl]),
                           sem_="s_xr%d" % xb, n=16)
                d_x2 = eop("gpsimd", [d_mg, d_xr, em.need(("x2", m))],
                           lambda e, m=m, xb=xb: e.tensor_add(
                               x2[:, m, :], x2[:, m, :], xto[xb][:]))
                em.release(("xto", xb), d_x2)
                x2_deps.append(d_x2)
            yield "x2"

            ln2_deps = emit_ln(x2, x2_deps, ln2, xbf2, xsq2, rows2,
                               xto, "ff", xto_key="xto")
            yield "ln2"

            d_umm_last = None
            h_deps = []
            upend = None

            def u_evac(m, d1, d2):
                b1 = pA if m % 2 == 0 else pB
                b2 = pS1 if m % 2 == 0 else pS2
                d_u1 = eop("scalar", [d1, em.need("u1t")],
                           lambda e, m=m, b1=b1: e.activation(
                               u1t[0][:], b1[:, 0:TC], AF.Silu,
                               scale=vw1s[:, m:m + 1]))
                em.release(("p4", m % 2), d_u1)
                d_u2 = eop("scalar", [d2, em.need("u2t")],
                           lambda e, m=m, b2=b2: e.activation(
                               u2t[0][:], b2[:, 0:TC],
                               AF.Identity, scale=vw2s[:, m:m + 1]))
                em.release(("p4", 2 + m % 2), d_u2)
                d_h = eop("gpsimd", [d_u1, d_u2, em.need(("hbuf", m))],
                          lambda e, m=m: e.tensor_mul(hbuf[:, m, :],
                                                      u1t[0][:], u2t[0][:]))
                em.release("u1t", d_h)
                em.release("u2t", d_h)
                h_deps.append(d_h)

            for m in range(NMH):
                b1 = pA if m % 2 == 0 else pB
                b2 = pS1 if m % 2 == 0 else pS2
                d_w1 = eop("sync", [em.need(("wt_s", 0)), ffn_cast_done],
                           lambda e, m=m: e.dma_start(
                               out=wt_s[0][:],
                               in_=wb_1[m].rearrange("p a b -> p (a b)")),
                           sem_="s_ws0", n=16)
                d_mm1 = egroup("tensor",
                               [d_w1, em.need(("p4", m % 2))] +
                               (ln2_deps if m == 0 else []),
                               lambda e, m=m, b1=b1: [
                                   e.matmul(b1[:, 0:TC],
                                            wt_s[0][:, k * 128:(k + 1) * 128],
                                            ln2[:, k, :], start=(k == 0),
                                            stop=(k == NKD - 1))
                                   for k in range(NKD)])
                em.release(("wt_s", 0), d_mm1)
                d_w2 = eop("sync", [em.need(("wt_s", 1)), ffn_cast_done],
                           lambda e, m=m: e.dma_start(
                               out=wt_s[1][:],
                               in_=wb_2[m].rearrange("p a b -> p (a b)")),
                           sem_="s_ws1", n=16)
                d_mm2 = egroup("tensor",
                               [d_w2, em.need(("p4", 2 + m % 2))] +
                               (ln2_deps if m == 0 else []),
                               lambda e, m=m, b2=b2: [
                                   e.matmul(b2[:, 0:TC],
                                            wt_s[1][:, k * 128:(k + 1) * 128],
                                            ln2[:, k, :], start=(k == 0),
                                            stop=(k == NKD - 1))
                                   for k in range(NKD)])
                em.release(("wt_s", 1), d_mm2)
                d_umm_last = d_mm2
                if upend is not None:
                    u_evac(*upend)
                upend = (m, d_mm1, d_mm2)
                yield "u"
            u_evac(*upend)
            for k in range(NKD):
                em.release(("ffln", k), d_umm_last)

            def big_layer(wb_src, nmt, rhs_get, rhs_deps, evac, first_dep):
                d_mm = None
                outs = []
                for m in range(nmt):
                    wbuf = m % 2
                    d_w = eop("sync", [em.need(("wt_h", wbuf)), ffn_cast_done],
                              lambda e, wb_src=wb_src, m=m, wbuf=wbuf:
                              e.dma_start(out=wt_h[wbuf][:],
                                          in_=wb_src[m].rearrange(
                                              "p a b -> p (a b)")),
                              sem_="s_wh%d" % wbuf, n=16)
                    d_mm = egroup("tensor",
                                  [d_w, first_dep if m == 0 else None,
                                   em.need(("pAB", m % 2))] +
                                  (rhs_deps if m == 0 else []),
                                  lambda e, m=m, wbuf=wbuf: [
                                      e.matmul(
                                          (pA if m % 2 == 0 else pB)[:, 0:TC],
                                          wt_h[wbuf][:, k * 128:(k + 1) * 128],
                                          rhs_get(k), start=(k == 0),
                                          stop=(k == NMH - 1))
                                      for k in range(NMH)])
                    em.release(("wt_h", wbuf), d_mm)
                    outs.append(evac(m, d_mm))
                return outs, d_mm

            def evac_h3(m, d_mm):
                d_e = eop("scalar", [d_mm, em.need(("h3buf", m))],
                          lambda e, m=m: e.activation(
                              h3buf[:, m, :],
                              (pA if m % 2 == 0 else pB)[:, 0:TC], AF.Gelu,
                              scale=vh1s[:, m:m + 1]))
                em.release(("p4", m % 2), d_e)
                return d_e

            def evac_h4(m, d_mm):
                d_e = eop("scalar", [d_mm, em.need(("hbuf", m))],
                          lambda e, m=m: e.activation(
                              hbuf[:, m, :],
                              (pA if m % 2 == 0 else pB)[:, 0:TC], AF.Gelu,
                              scale=vh2s[:, m:m + 1]))
                em.release(("p4", m % 2), d_e)
                return d_e

            def evac_out(m, d_mm):
                d_e = eop("scalar", [d_mm, em.need("u1t")],
                          lambda e, m=m: e.activation(
                              u1t[0][:],
                              (pA if m % 2 == 0 else pB)[:, 0:TC],
                              AF.Identity, scale=vw3s[:, m:m + 1]))
                em.release(("p4", m % 2), d_e)
                d_o = eop("gpsimd", [d_e, x2_deps[m], em.need(("xto", m % 2))],
                          lambda e, m=m: e.tensor_add(xto[m % 2][:],
                                                      u1t[0][:],
                                                      x2[:, m, :]))
                em.release("u1t", d_o)
                em.release(("x2", m), d_o)
                x2_read_deps.append(d_o)
                d_os = eop("gpsimd", [d_o],
                           lambda e, m=m, fb=fb: e.dma_start(
                               out=out_fm[m * 128:(m + 1) * 128,
                                          fb * TC:(fb + 1) * TC],
                               in_=xto[m % 2][:]),
                           sem_="s_o%d" % (m % 2), n=16)
                em.release(("xto", m % 2), d_os)
                return d_os

            x2_read_deps = []
            h3_deps = []
            d_h1_last = None
            hpend = None
            for m in range(NMH):
                wbuf = m % 2
                d_w = eop("sync", [em.need(("wt_h", wbuf)), ffn_cast_done],
                          lambda e, m=m, wbuf=wbuf: e.dma_start(
                              out=wt_h[wbuf][:],
                              in_=wb_h1[m].rearrange("p a b -> p (a b)")),
                          sem_="s_wh%d" % wbuf, n=16)
                d_mm = egroup("tensor",
                              [d_w, em.need(("p4", m % 2))] +
                              (h_deps if m == 0 else []),
                              lambda e, m=m, wbuf=wbuf: [
                                  e.matmul((pA if m % 2 == 0 else pB)[:, 0:TC],
                                           wt_h[wbuf][:, k * 128:(k + 1) * 128],
                                           hbuf[:, k, :], start=(k == 0),
                                           stop=(k == NMH - 1))
                                  for k in range(NMH)])
                em.release(("wt_h", wbuf), d_mm)
                d_h1_last = d_mm
                if hpend is not None:
                    h3_deps.append(evac_h3(*hpend))
                hpend = (m, d_mm)
                yield "h1"
            h3_deps.append(evac_h3(*hpend))
            for m in range(NMH):
                em.release(("hbuf", m), d_h1_last)

            h4_deps = []
            d_h2_last = None
            hpend = None
            for m in range(NMH):
                wbuf = m % 2
                d_w = eop("sync", [em.need(("wt_h", wbuf)), ffn_cast_done],
                          lambda e, m=m, wbuf=wbuf: e.dma_start(
                              out=wt_h[wbuf][:],
                              in_=wb_h2[m].rearrange("p a b -> p (a b)")),
                          sem_="s_wh%d" % wbuf, n=16)
                d_mm = egroup("tensor",
                              [d_w, em.need(("p4", m % 2))] +
                              (h3_deps if m == 0 else []),
                              lambda e, m=m, wbuf=wbuf: [
                                  e.matmul((pA if m % 2 == 0 else pB)[:, 0:TC],
                                           wt_h[wbuf][:, k * 128:(k + 1) * 128],
                                           h3buf[:, k, :], start=(k == 0),
                                           stop=(k == NMH - 1))
                                  for k in range(NMH)])
                em.release(("wt_h", wbuf), d_mm)
                d_h2_last = d_mm
                if hpend is not None:
                    h4_deps.append(evac_h4(*hpend))
                hpend = (m, d_mm)
                yield "h2"
            h4_deps.append(evac_h4(*hpend))
            for m in range(NMH):
                em.release(("h3buf", m), d_h2_last)

            d_w3_last = None
            hpend = None
            for m in range(NKD):
                wbuf = m % 2
                d_w = eop("sync", [em.need(("wt_h", wbuf)), ffn_cast_done],
                          lambda e, m=m, wbuf=wbuf: e.dma_start(
                              out=wt_h[wbuf][:],
                              in_=wb_3[m].rearrange("p a b -> p (a b)")),
                          sem_="s_wh%d" % wbuf, n=16)
                d_mm = egroup("tensor",
                              [d_w, em.need(("p4", m % 2))] +
                              (h4_deps if m == 0 else []),
                              lambda e, m=m, wbuf=wbuf: [
                                  e.matmul((pA if m % 2 == 0 else pB)[:, 0:TC],
                                           wt_h[wbuf][:, k * 128:(k + 1) * 128],
                                           hbuf[:, k, :], start=(k == 0),
                                           stop=(k == NMH - 1))
                                  for k in range(NMH)])
                em.release(("wt_h", wbuf), d_mm)
                d_w3_last = d_mm
                if hpend is not None:
                    evac_out(*hpend)
                hpend = (m, d_mm)
                yield "w3"
            evac_out(*hpend)
            em.release("x2w", x2_read_deps)
            for m in range(NMH):
                em.release(("hbuf", m), d_w3_last)

        NSC = NCH if PH >= 3 else 0
        NFB = NSC // 2 if PH >= 4 else 0
        FFN_UNITS_PER_YIELD = 1
        active_ffn = None

        def drive_ffn(k):
            nonlocal_done = False
            if active_ffn is None:
                return
            for _ in range(k):
                if next(active_ffn, "__done__") == "__done__":
                    break

        for fb in range((NSC + 1) // 2):
            for cidx in (2 * fb, 2 * fb + 1):
                if cidx >= NSC:
                    break
                for unit in m2_chunk_stream(cidx, first=(cidx == 0)):
                    drive_ffn(FFN_UNITS_PER_YIELD)
            if fb < NFB:
                if active_ffn is not None:
                    for _ in active_ffn:
                        pass
                active_ffn = ffn_block_stream(fb)
        if active_ffn is not None:
            for _ in active_ffn:
                pass

        final_waits = [(n, em.val(n)) for n in list(S.keys())
                       if n not in ENGINE_SEMS and em.val(n) > 0]

        @em.on("gpsimd")
        def _(e, final_waits=final_waits):
            for n, v in final_waits:
                e.wait_ge(S[n], v)

        @block.sync
        def _(eng):
            for fn in em.q["sync"]:
                fn(eng)

        @block.tensor
        def _(eng):
            for fn in em.q["tensor"]:
                fn(eng)

        @block.vector
        def _(eng):
            for fn in em.q["vector"]:
                fn(eng)

        @block.scalar
        def _(eng):
            for fn in em.q["scalar"]:
                fn(eng)

        @block.gpsimd
        def _(eng):
            for fn in em.q["gpsimd"]:
                fn(eng)

    return nc


def kernel(**inputs):
    ii = {k: np.asarray(v, dtype=np.float32) for k, v in inputs.items()}
    x = ii["L0_attn_x"]
    W_in, W_x = ii["L0_attn_W_in"], ii["L0_attn_W_x"]
    W_dt, W_out = ii["L0_attn_W_dt"], ii["L0_attn_W_out"]
    conv_w, conv_b = ii["L0_attn_conv_w"], ii["L0_attn_conv_b"]
    b_dt, Dv = ii["L0_attn_b_dt"], ii["L0_attn_D"]

    nc = build_program()

    shared = {
        "w_1": pack_w(ii["L0_ffn_w1_w"]),
        "w_2": pack_w(ii["L0_ffn_w2_w"]),
        "w_h1": pack_w(ii["L0_ffn_h1_w"]),
        "w_h2": pack_w(ii["L0_ffn_h2_w"]),
        "w_3": pack_w(ii["L0_ffn_w3_w"]),
        "a_log": ii["L0_attn_A_log"][None, :],
        "w1s_p": pack_vec(ii["L0_ffn_w1_s"]),
        "w2s_p": pack_vec(ii["L0_ffn_w2_s"]),
        "h1s_p": pack_vec(ii["L0_ffn_h1_s"]),
        "h2s_p": pack_vec(ii["L0_ffn_h2_s"]),
        "w3s_p": pack_vec(ii["L0_ffn_w3_s"]),
    }

    in_maps = []
    for c in range(8):
        b, j = c // 2, c % 2
        ch = slice(j * CH, (j + 1) * CH)
        m = dict(shared)
        xt = np.ascontiguousarray(x[b].T)
        m["x_fm"] = xt
        m["x_res"] = np.ascontiguousarray(
            xt.reshape(DM, L // 64, 64)[:, j::2, :].reshape(DM, L // 2))
        m["w_in"] = pack_w(np.ascontiguousarray(
            np.concatenate([W_in[ch], W_in[2048 + j * CH:2048 + (j + 1) * CH]],
                           axis=0)))
        m["w_x"] = pack_w(np.ascontiguousarray(W_x[:, ch]))
        m["w_dtp"] = pack_w(np.ascontiguousarray(W_dt[ch]))
        m["w_out"] = pack_w(np.ascontiguousarray(W_out[:, ch]))
        m["conv_w_p"] = np.ascontiguousarray(
            conv_w[ch].reshape(NDT, 128, DCONV).transpose(1, 0, 2))
        m["conv_b_p"] = pack_vec(conv_b[ch])
        m["b_dt_p"] = pack_vec(b_dt[ch])
        m["d_p"] = pack_vec(Dv[ch])
        in_maps.append(m)

    res = run_bass_kernel_spmd(nc, in_maps, core_ids=list(range(8)))
    out = np.empty((B, L, DM), dtype=np.float32)
    for b in range(B):
        ofm = np.empty((DM, L), dtype=np.float32)
        v = ofm.reshape(DM, L // 64, 64)
        v[:, 0::2, :] = res.results[2 * b]["out_fm"].reshape(DM, L // 128, 64)
        v[:, 1::2, :] = res.results[2 * b + 1]["out_fm"].reshape(DM, L // 128, 64)
        out[b] = ofm.T
    return out


# version 20 (final)
# speedup vs baseline: 1.0549x; 1.0549x over previous
import numpy as np
import concourse.bass as bass
import concourse.mybir as mybir
from concourse.bass_utils import run_bass_kernel_spmd

F32 = mybir.dt.float32
BF16 = mybir.dt.bfloat16
AF = mybir.ActivationFunctionType
ALU = mybir.AluOpType

B, L, DM = 4, 8192, 1024
DI, DS, DTR, DCONV, DH = 2048, 16, 64, 4, 4096

CH = 1024
NKD = DM // 128
NDT = CH // 128
NMH = DH // 128
TC = 512
NCHUNK = L // TC
EPS = 1e-5
ARENA_ELEMS = 52480


def pack_w(W):
    M, K = W.shape
    mt, kt = max(1, M // 128), max(1, K // 128)
    mm, kk = min(128, M), min(128, K)
    Wr = W.reshape(mt, mm, kt, kk)
    return np.ascontiguousarray(Wr.transpose(0, 3, 2, 1))


def pack_vec(v):
    n = v.shape[0]
    return np.ascontiguousarray(v.reshape(n // 128, 128).T)


class Emit:
    ENGS = ("sync", "tensor", "vector", "scalar", "gpsimd")

    def __init__(self):
        self.q = {e: [] for e in self.ENGS}
        self.sem = {}
        self.rel = {}
        self.default_rel = None

    def phase_barrier(self):
        self.default_rel = [(n, v) for n, v in self.sem.items() if v > 0]

    def inc(self, name, n=1):
        self.sem[name] = self.sem.get(name, 0) + n
        return self.sem[name]

    def val(self, name):
        return self.sem.get(name, 0)

    def on(self, eng):
        def deco(fn):
            self.q[eng].append(fn)
        return deco

    def release(self, key, deps):
        if isinstance(deps, tuple):
            deps = [deps]
        self.rel[key] = list(deps)

    def need(self, key):
        return self.rel.get(key, self.default_rel)


class Arena:
    def __init__(self, tens, n_f32):
        self.t, self.size, self.off = tens, n_f32, 0

    def take(self, shape, dtype=F32):
        n = int(np.prod(shape))
        nf = n if dtype == F32 else (n + 1) // 2
        assert self.off + nf <= self.size, (self.off, nf, self.size)
        ap = self.t[:, self.off:self.off + nf]
        self.off += nf
        if dtype == BF16:
            ap = ap.bitcast(BF16)
        if len(shape) == 2:
            ap = ap.rearrange("p (a b) -> p a b", a=shape[0])
        elif len(shape) == 3:
            ap = ap.rearrange("p (a b c) -> p a b c", a=shape[0], b=shape[1])
        return ap

    def mark(self):
        return self.off

    def reset(self, off=0):
        self.off = off


ENGINE_SEMS = frozenset(["s_pe", "s_ac", "s_dv", "s_gp"])


def _flat_deps(deps, em=None):
    out = {}
    stack = list(deps)
    while stack:
        d = stack.pop()
        if d is None:
            continue
        if isinstance(d, list):
            stack.extend(d)
            continue
        sname, sval = d
        if em is not None and sname not in ENGINE_SEMS:
            sval = em.val(sname)
        if sval > out.get(sname, 0):
            out[sname] = sval
    return tuple(out.items())


def build_program():
    import os
    PH = int(os.environ.get("KPHASES", "4"))
    NCH = int(os.environ.get("KNCHUNK", str(NCHUNK)))
    nc = bass.Bass("TRN2", target_bir_lowering=False, debug=False)

    def din(name, shape):
        return nc.dram_tensor(name, shape, F32, kind="ExternalInput").ap()

    x_fm = din("x_fm", [DM, L])
    x_res = din("x_res", [DM, L // 2])
    w_in = din("w_in", [2 * NDT, 128, NKD, 128])
    w_x = din("w_x", [1, 128, NDT, 96])
    w_dtp = din("w_dtp", [NDT, 64, 1, 128])
    w_out = din("w_out", [NKD, 128, NDT, 128])
    w_1 = din("w_1", [NMH, 128, NKD, 128])
    w_2 = din("w_2", [NMH, 128, NKD, 128])
    w_h1 = din("w_h1", [NMH, 128, NMH, 128])
    w_h2 = din("w_h2", [NMH, 128, NMH, 128])
    w_3 = din("w_3", [NKD, 128, NMH, 128])
    conv_w_p = din("conv_w_p", [128, NDT, DCONV])
    conv_b_p = din("conv_b_p", [128, NDT])
    b_dt_p = din("b_dt_p", [128, NDT])
    d_p = din("d_p", [128, NDT])
    a_log = din("a_log", [1, DS])
    w1s_p = din("w1s_p", [128, NMH])
    w2s_p = din("w2s_p", [128, NMH])
    h1s_p = din("h1s_p", [128, NMH])
    h2s_p = din("h2s_p", [128, NMH])
    w3s_p = din("w3s_p", [128, NKD])

    out_fm = nc.dram_tensor("out_fm", [DM, L // 2], F32,
                            kind="ExternalOutput").ap()

    def scr(name, shape, dt=F32):
        return nc.dram_tensor(name, shape, dt, kind="Internal").ap()

    wb_in = scr("wb_in", [2 * NDT, 128, NKD, 128], BF16)
    wb_x = scr("wb_x", [1, 128, NDT, 96], BF16)
    wb_dtp = scr("wb_dtp", [NDT, 64, 1, 128], BF16)
    wb_out = scr("wb_out", [NKD, 128, NDT, 128], BF16)
    wb_1 = scr("wb_1", [NMH, 128, NKD, 128], BF16)
    wb_2 = scr("wb_2", [NMH, 128, NKD, 128], BF16)
    wb_h1 = scr("wb_h1", [NMH, 128, NMH, 128], BF16)
    wb_h2 = scr("wb_h2", [NMH, 128, NMH, 128], BF16)
    wb_3 = scr("wb_3", [NKD, 128, NMH, 128], BF16)

    xs_dram = scr("xs_dram", [NDT, 128, L], F32)
    z_dram = scr("z_dram", [NDT, 128, L], BF16)
    dt_dram = scr("dt_dram", [NDT, 128, L], F32)
    xdblb_dram = scr("xdblb_dram", [NCHUNK, 32, TC], BF16)
    xag_in = scr("xag_in", [NCHUNK, 96, TC], F32)
    xag_out = scr("xag_out", [NCHUNK, 2, 96, TC], F32)
    rs_in = scr("rs_in", [NCHUNK, 2, NKD, 128, TC // 2], F32)
    rs_out = scr("rs_out", [NCHUNK, NKD, 128, TC // 2], F32)

    em = Emit()

    from contextlib import ExitStack
    with ExitStack() as _es:
        def _e(cm):
            return _es.enter_context(cm)
        S = {}

        def sem(name):
            if name not in S:
                S[name] = _e(nc.semaphore(name))
            return S[name]
        for _n in ("s_pe", "s_ac", "s_dv"):
            sem(_n)
        pA = _e(nc.psum_tensor("pA", [128, 512], F32))
        pB = _e(nc.psum_tensor("pB", [128, 512], F32))
        pS1 = _e(nc.psum_tensor("pS1", [128, 512], F32))
        pS2 = _e(nc.psum_tensor("pS2", [128, 512], F32))
        pC1 = _e(nc.psum_tensor("pC1", [128, 512], F32))
        pC2 = _e(nc.psum_tensor("pC2", [128, 512], F32))
        pC3 = _e(nc.psum_tensor("pC3", [128, 512], F32))
        pC4 = _e(nc.psum_tensor("pC4", [128, 512], F32))
        ones_col = _e(nc.sbuf_tensor("ones_col", [128, 1], BF16))
        ones_row = _e(nc.sbuf_tensor("ones_row", [1, 128], F32))
        ones_bf = _e(nc.sbuf_tensor("ones_bf", [1, 128], BF16))
        vec_convw = _e(nc.sbuf_tensor("vec_convw", [128, NDT, DCONV], F32))
        vec_convb = _e(nc.sbuf_tensor("vec_convb", [128, NDT], F32))
        vec_bdt = _e(nc.sbuf_tensor("vec_bdt", [128, NDT], F32))
        vec_d = _e(nc.sbuf_tensor("vec_d", [128, NDT], F32))
        vw1s = _e(nc.sbuf_tensor("vw1s", [128, NMH], F32))
        vw2s = _e(nc.sbuf_tensor("vw2s", [128, NMH], F32))
        vh1s = _e(nc.sbuf_tensor("vh1s", [128, NMH], F32))
        vh2s = _e(nc.sbuf_tensor("vh2s", [128, NMH], F32))
        vw3s = _e(nc.sbuf_tensor("vw3s", [128, NKD], F32))
        a_row = _e(nc.sbuf_tensor("a_row", [1, DS], F32))
        a_cols = _e(nc.sbuf_tensor("a_cols", [128, DS], F32))
        state = _e(nc.sbuf_tensor("state", [128, NDT, DS], F32))
        halo = _e(nc.sbuf_tensor("halo", [128, NDT, 3], F32))
        eps_col = _e(nc.sbuf_tensor("eps_col", [128, 1], F32))
        arena_t = _e(nc.sbuf_tensor("arena", [128, ARENA_ELEMS], F32))
        block = _e(nc.Block())

        ar = Arena(arena_t, ARENA_ELEMS)
        ESEM = {"tensor": "s_pe", "vector": "s_dv", "scalar": "s_ac",
                "gpsimd": "s_gp"}

        def eop(eng, deps, build, sem_=None, n=1):
            semname = sem_ or ESEM[eng]
            sem(semname)
            for d in _flat_deps(deps, em):
                sem(d[0])
            v = em.inc(semname, n)
            dd = _flat_deps(deps, em)
            for sname, sval in dd:
                assert sval <= em.val(sname), (sname, sval, em.val(sname))

            @em.on(eng)
            def _(e, build=build, dd=dd, semname=semname, n=n):
                for sname, sval in dd:
                    e.wait_ge(S[sname], sval)
                build(e).then_inc(S[semname], n)
            return (semname, v)

        def egroup(eng, deps, build, sem_=None, tag_last=1):
            semname = sem_ or ESEM[eng]
            v = em.inc(semname, tag_last)
            dd = _flat_deps(deps, em)
            for sname, sval in dd:
                assert sval <= em.val(sname), (sname, sval, em.val(sname))

            @em.on(eng)
            def _(e, build=build, dd=dd, semname=semname, tag_last=tag_last):
                for sname, sval in dd:
                    e.wait_ge(S[sname], sval)
                instrs = build(e)
                if not isinstance(instrs, (list, tuple)):
                    instrs = [instrs]
                for ins in instrs[-tag_last:]:
                    ins.then_inc(S[semname], 1)
            return (semname, v)

        def edma(eng, deps, build, sem_, count):
            sem(sem_)
            for d in _flat_deps(deps, em):
                sem(d[0])
            v = em.inc(sem_, 16 * count)
            dd = _flat_deps(deps, em)
            for sname, sval in dd:
                assert sval <= em.val(sname), (sname, sval, em.val(sname))

            @em.on(eng)
            def _(e, build=build, dd=dd, sem_=sem_):
                for sname, sval in dd:
                    e.wait_ge(S[sname], sval)
                for ins in build(e):
                    ins.then_inc(S[sem_], 16)
            return (sem_, v)

        d_setup = edma("sync", [], lambda e: [
            e.dma_start(out=vec_convw[:], in_=conv_w_p),
            e.dma_start(out=vec_convb[:], in_=conv_b_p),
            e.dma_start(out=vec_bdt[:], in_=b_dt_p),
            e.dma_start(out=vec_d[:], in_=d_p),
            e.dma_start(out=vw1s[:], in_=w1s_p),
            e.dma_start(out=vw2s[:], in_=w2s_p),
            e.dma_start(out=vh1s[:], in_=h1s_p),
            e.dma_start(out=vh2s[:], in_=h2s_p),
            e.dma_start(out=vw3s[:], in_=w3s_p),
            e.dma_start(out=a_row[:], in_=a_log)], "s_ld", 10)

        d_init = egroup("vector", [], lambda e: (
            e.memset(ones_col[:], 1.0),
            e.memset(ones_row[:], 1.0),
            e.memset(ones_bf[:], 1.0),
            e.memset(eps_col[:], EPS),
            e.memset(state[:], 0.0),
            e.memset(halo[:], 0.0)), tag_last=5)

        d_ar1 = eop("scalar", [d_setup],
                    lambda e: e.activation(a_row[:], a_row[:], AF.Exp))
        d_ar2 = eop("scalar", [d_ar1],
                    lambda e: e.mul(a_row[:], a_row[:], -1.0))
        d_acm = eop("tensor", [d_ar2, d_init],
                    lambda e: e.matmul(pA[:, 0:DS], ones_row[:], a_row[:],
                                       start=True, stop=True))
        d_acols = eop("scalar", [d_acm],
                      lambda e: e.copy(a_cols[:], pA[:, 0:DS]))
        em.release(("p4", 0), d_acols)

        wc_f = [ar.take([4096]), ar.take([4096])]
        wc_b = [ar.take([4096], BF16), ar.take([4096], BF16)]
        casts = [
            (w_in, wb_in, 2 * NDT, NKD * 128, 128),
            (w_x, wb_x, 1, NDT * 96, 128),
            (w_dtp, wb_dtp, NDT, 128, 64),
            (w_out, wb_out, NKD, NDT * 128, 128),
        ]
        cidx = 0
        for (src, dst, nmt, row, part) in casts:
            for mt in range(nmt):
                bi = cidx % 2
                cidx += 1
                d_l = eop("sync", [em.need(("wc_f", bi))],
                          lambda e, src=src, mt=mt, bi=bi, row=row, part=part:
                          e.dma_start(out=wc_f[bi][0:part, 0:row],
                                      in_=src[mt].rearrange("p a b -> p (a b)")),
                          sem_="s_wcf%d" % bi, n=16)
                d_c = eop("vector", [d_l, em.need(("wc_b", bi))],
                          lambda e, bi=bi, row=row, part=part:
                          e.tensor_copy(wc_b[bi][0:part, 0:row],
                                        wc_f[bi][0:part, 0:row]))
                em.release(("wc_f", bi), d_c)
                d_s = eop("gpsimd", [d_c],
                          lambda e, dst=dst, mt=mt, bi=bi, row=row, part=part:
                          e.dma_start(out=dst[mt].rearrange("p a b -> p (a b)"),
                                      in_=wc_b[bi][0:part, 0:row]),
                          sem_="s_wcb%d" % bi, n=16)
                em.release(("wc_b", bi), d_s)
        m0_done = [("s_wcb0", em.val("s_wcb0")), ("s_wcb1", em.val("s_wcb1"))]
        ar.reset(0)
        em.phase_barrier()

        x_sb = ar.take([NKD, TC])
        tmp_f = [ar.take([TC]), ar.take([TC])]
        xr_t = [ar.take([TC + 4]), ar.take([TC + 4])]
        conv_t = ar.take([TC])
        cv4 = ar.take([TC])
        xs_f = ar.take([NDT, TC])
        xdbl = ar.take([TC])
        xd2 = ar.take([2, TC])
        xdbl_f = ar.take([TC])
        sp_na = ar.take([TC])
        sp_e = ar.take([TC])
        sp_m = [ar.take([TC]), ar.take([TC])]
        sp_l = [ar.take([TC]), ar.take([TC])]
        dt_t = [ar.take([TC]), ar.take([TC]), ar.take([TC]), ar.take([TC])]
        rows = ar.take([4, TC])
        xbf = [ar.take([TC], BF16), ar.take([TC], BF16)]
        xsq = [ar.take([TC], BF16), ar.take([TC], BF16)]
        ln1 = ar.take([NKD, TC], BF16)
        xs_bf = ar.take([NDT, TC], BF16)
        z_t = [ar.take([TC], BF16), ar.take([TC], BF16),
               ar.take([TC], BF16), ar.take([TC], BF16)]
        xdbl_bf = ar.take([TC], BF16)
        wt_in = ar.take([2 * NDT * NKD * 128], BF16)
        wt_x = ar.take([NDT * 96], BF16)
        wt_dt = ar.take([NDT * 128], BF16)
        wc2_f = [ar.take([4096]), ar.take([4096])]
        wc2_b = [ar.take([4096], BF16), ar.take([4096], BF16)]
        assert ar.mark() <= ARENA_ELEMS, ar.mark()

        def ffn_cast_stream():
            fcasts = [
                (w_1, wb_1, NMH, NKD * 128, 128),
                (w_2, wb_2, NMH, NKD * 128, 128),
                (w_h1, wb_h1, NMH, DH, 128),
                (w_h2, wb_h2, NMH, DH, 128),
                (w_3, wb_3, NKD, DH, 128),
            ]
            ci2 = 0
            for (srct, dst, nmt, row, part) in fcasts:
                for mt in range(nmt):
                    bi = ci2 % 2
                    ci2 += 1
                    d_l = eop("sync", [em.need(("wc2_f", bi))],
                              lambda e, srct=srct, mt=mt, bi=bi, row=row:
                              e.dma_start(out=wc2_f[bi][0:128, 0:row],
                                          in_=srct[mt].rearrange(
                                              "p a b -> p (a b)")),
                              sem_="s_wc2f%d" % bi, n=16)
                    d_c = eop("scalar", [d_l, em.need(("wc2_b", bi))],
                              lambda e, bi=bi, row=row:
                              e.copy(wc2_b[bi][0:128, 0:row],
                                     wc2_f[bi][0:128, 0:row]))
                    em.release(("wc2_f", bi), d_c)
                    d_s = eop("gpsimd", [d_c],
                              lambda e, dst=dst, mt=mt, bi=bi, row=row:
                              e.dma_start(out=dst[mt].rearrange(
                                              "p a b -> p (a b)"),
                                          in_=wc2_b[bi][0:128, 0:row]),
                              sem_="s_wc2b%d" % bi, n=16)
                    em.release(("wc2_b", bi), d_s)
                    yield "cast"

        def emit_ln(x_view, x_dep, ln_out, xbf_, xsq_, rows_, tmp_, kp,
                    xto_key=None):
            nb, nq, nt = len(xbf_), len(xsq_), len(tmp_)
            tkey = (lambda k: (xto_key, k % nt)) if xto_key else (
                lambda k: (kp + "tmp", k % nt))
            d_mm = None
            for k in range(NKD):
                xd = x_dep[k] if isinstance(x_dep, list) else x_dep
                d_cb = eop("vector", [xd, em.need((kp + "xbf", k % nb)), d_init],
                           lambda e, k=k: e.tensor_copy(xbf_[k % nb][:],
                                                        x_view[:, k, :]))
                d_sq = eop("scalar", [xd, em.need((kp + "xsq", k % nq))],
                           lambda e, k=k: e.activation(xsq_[k % nq][:],
                                                       x_view[:, k, :],
                                                       AF.Square))
                d_mm = egroup("tensor",
                              [d_cb, d_sq,
                               em.need(("p4", 2)) if k == 0 else None,
                               em.need(("p4", 3)) if k == 0 else None],
                              lambda e, k=k: (
                                  e.matmul(pS1[0:1, 0:TC], ones_col[:],
                                           xbf_[k % nb][:], start=(k == 0),
                                           stop=(k == NKD - 1)),
                                  e.matmul(pS2[0:1, 0:TC], ones_col[:],
                                           xsq_[k % nq][:], start=(k == 0),
                                           stop=(k == NKD - 1))),
                              tag_last=2)
                em.release((kp + "xbf", k % nb), d_mm)
                em.release((kp + "xsq", k % nq), d_mm)

            rw = em.need(kp + "rows")
            d_mean = eop("vector", [d_mm, rw],
                         lambda e: e.tensor_scalar_mul(rows_[0:1, 0, :],
                                                       pS1[0:1, 0:TC], 1.0 / DM))
            d_msq = eop("vector", [d_mm, rw],
                        lambda e: e.tensor_scalar_mul(rows_[0:1, 1, :],
                                                      pS2[0:1, 0:TC], 1.0 / DM))
            d_m2 = eop("vector", [d_mean],
                       lambda e: e.tensor_mul(rows_[0:1, 2, :], rows_[0:1, 0, :],
                                              rows_[0:1, 0, :]))
            d_var = eop("vector", [d_msq, d_m2],
                        lambda e: e.tensor_sub(rows_[0:1, 1, :],
                                               rows_[0:1, 1, :],
                                               rows_[0:1, 2, :]))
            d_ln = eop("scalar", [d_var],
                       lambda e: e.activation(rows_[0:1, 2, :], rows_[0:1, 1, :],
                                              AF.Ln, bias=eps_col[0:1, :]))
            d_rstd = eop("scalar", [d_ln],
                         lambda e: e.activation(rows_[0:1, 1, :],
                                                rows_[0:1, 2, :],
                                                AF.Exp, scale=-0.5))
            d_B = eop("vector", [d_rstd],
                      lambda e: e.tensor_mul(rows_[0:1, 2, :], rows_[0:1, 0, :],
                                             rows_[0:1, 1, :]))
            d_bc = egroup("tensor", [d_B],
                          lambda e: (
                              e.matmul(pS1[:, 0:TC], ones_row[:],
                                       rows_[0:1, 1, :], start=True, stop=True),
                              e.matmul(pS2[:, 0:TC], ones_row[:],
                                       rows_[0:1, 2, :], start=True,
                                       stop=True)), tag_last=2)
            em.release(kp + "rows", d_bc)
            norm_deps = []
            for k in range(NKD):
                xd = x_dep[k] if isinstance(x_dep, list) else x_dep
                d_t = eop("vector", [d_bc, xd, em.need(tkey(k))],
                          lambda e, k=k: e.tensor_mul(tmp_[k % nt][:],
                                                      x_view[:, k, :],
                                                      pS1[:, 0:TC]))
                d_n = eop("vector", [d_t, em.need((kp + "ln", k))],
                          lambda e, k=k: e.tensor_sub(ln_out[:, k, :],
                                                      tmp_[k % nt][:],
                                                      pS2[:, 0:TC]))
                em.release(tkey(k), d_n)
                norm_deps.append(d_n)
            em.release(("p4", 2), norm_deps[-1])
            em.release(("p4", 3), norm_deps[-1])
            return norm_deps

        def m1_stream():
          d_wi = edma("sync", [m0_done],
                      lambda e: [e.dma_start(
                          out=wt_in[:, m * NKD * 128:(m + 1) * NKD * 128],
                          in_=wb_in[m].rearrange("p a b -> p (a b)"))
                          for m in range(2 * NDT)],
                      "s_wi", 2 * NDT)
          d_wx = eop("sync", [m0_done],
                     lambda e: e.dma_start(
                         out=wt_x[:],
                         in_=wb_x[0].rearrange("p a b -> p (a b)")),
                     sem_="s_wx", n=16)
          d_wdt = eop("sync", [m0_done],
                      lambda e: e.dma_start(
                          out=wt_dt[0:64, :].rearrange("p (a b) -> p a b",
                                                       a=NDT),
                          in_=wb_dtp.rearrange("m k o c -> k m (o c)")),
                      sem_="s_wdt2", n=16)

          def part_a(ci):
            d_x = eop("sync", [em.need("x_sb")],
                      lambda e, ci=ci: e.dma_start(
                          out=x_sb[:],
                          in_=x_fm[:, ci * TC:(ci + 1) * TC].rearrange(
                              "(k p) t -> p k t", p=128)),
                      sem_="s_x", n=16)

            ln1_deps = emit_ln(x_sb, d_x, ln1, xbf, xsq, rows, tmp_f, "m1")
            em.release("x_sb", ln1_deps[-1])
            d_ln1 = ln1_deps[-1]
            yield "m1"

            xs_cast_deps = []
            d_xz_last = None
            xz_order = [t for pair in zip(range(NDT), range(NDT, 2 * NDT))
                        for t in pair]
            for pos, m in enumerate(xz_order):
                bk = pos % 2
                d_mm = egroup("tensor",
                              [d_wi if ci == 0 and pos == 0 else None,
                               d_ln1 if pos == 0 else None,
                               em.need(("pAB", bk))],
                              lambda e, m=m, bk=bk: [
                                  e.matmul((pA if bk == 0 else pB)[:, 0:TC],
                                           wt_in[:, (m * NKD + k) * 128:
                                                 (m * NKD + k + 1) * 128],
                                           ln1[:, k, :], start=(k == 0),
                                           stop=(k == NKD - 1))
                                  for k in range(NKD)])
                d_xz_last = d_mm

                if m < NDT:
                    d_ev = eop("scalar", [d_mm, em.need(("xr_t", m % 2))],
                               lambda e, m=m, bk=bk: e.copy(
                                   xr_t[m % 2][:, 3:3 + TC],
                                   (pA if bk == 0 else pB)[:, 0:TC]))
                    em.release(("pAB", bk), d_ev)

                    d_hi = eop("vector", [d_ev, em.need(("halo", m)), d_init],
                               lambda e, m=m: e.tensor_copy(
                                   xr_t[m % 2][:, 0:3], halo[:, m, :]))
                    d_c4 = eop("vector", [d_ev, em.need("cv4")],
                               lambda e, m=m: e.tensor_scalar_mul(
                                   cv4[:], xr_t[m % 2][:, 3:3 + TC],
                                   vec_convw[:, m, 3:4]))
                    d_cc = d_c4
                    for kk in range(3):
                        d_cc = eop("vector",
                                   [d_cc, d_hi,
                                    em.need("conv_t") if kk == 0 else None],
                                   lambda e, m=m, kk=kk: (
                                       e.scalar_tensor_tensor(
                                           conv_t[:],
                                           xr_t[m % 2][:, kk:kk + TC],
                                           vec_convw[:, m, kk:kk + 1],
                                           cv4[:] if kk == 0 else conv_t[:],
                                           ALU.mult, ALU.add)))
                    em.release("cv4", d_cc)
                    d_ho = eop("vector", [d_hi, d_ev],
                               lambda e, m=m: e.tensor_copy(
                                   halo[:, m, :], xr_t[m % 2][:, TC:TC + 3]))
                    em.release(("halo", m), d_ho)
                    em.release(("xr_t", m % 2), [d_cc, d_ho])

                    d_si = eop("scalar", [d_cc, em.need(("xs_f", m))],
                               lambda e, m=m: e.activation(
                                   xs_f[:, m, :], conv_t[:], AF.Silu,
                                   bias=vec_convb[:, m:m + 1]))
                    em.release("conv_t", d_si)

                    d_cb = eop("vector", [d_si, em.need(("xs_bf", m))],
                               lambda e, m=m: e.tensor_copy(
                                   xs_bf[:, m, :], xs_f[:, m, :]))
                    xs_cast_deps.append(d_cb)

                    d_sp = eop("gpsimd", [d_si],
                               lambda e, m=m, ci=ci: e.dma_start(
                                   out=xs_dram[m, :, ci * TC:(ci + 1) * TC],
                                   in_=xs_f[:, m, :]),
                               sem_="s_xs%d" % m, n=16)
                    em.release(("xs_f", m), [d_cb, d_sp])
                else:
                    zi = m - NDT
                    zb = zi % 4
                    d_ev = eop("scalar", [d_mm, em.need(("z_t", zb))],
                               lambda e, zb=zb, bk=bk: e.activation(
                                   z_t[zb][:],
                                   (pA if bk == 0 else pB)[:, 0:TC],
                                   AF.Identity))
                    em.release(("pAB", bk), d_ev)
                    d_sp = eop("gpsimd", [d_ev],
                               lambda e, zi=zi, zb=zb, ci=ci: e.dma_start(
                                   out=z_dram[zi, :, ci * TC:(ci + 1) * TC],
                                   in_=z_t[zb][:]),
                               sem_="s_z%d" % zb, n=16)
                    em.release(("z_t", zb), d_sp)
                if pos % 4 == 3:
                    yield "m1"
            for k in range(NKD):
                em.release(("m1ln", k), d_xz_last)

            d_xd = egroup("tensor",
                          [d_wx if ci == 0 else None,
                           em.need("pC3")] + xs_cast_deps,
                          lambda e: [
                              e.matmul(pC3[0:96, 0:TC],
                                       wt_x[:, k * 96:(k + 1) * 96],
                                       xs_bf[:, k, :], start=(k == 0),
                                       stop=(k == NDT - 1))
                              for k in range(NDT)])
            for m in range(NDT):
                em.release(("xs_bf", m), d_xd)

            d_xde = eop("scalar", [d_xd, em.need("xdbl")],
                        lambda e: e.copy(xdbl[0:96, :], pC3[0:96, 0:TC]))
            em.release("pC3", d_xde)
            d_xsp = eop("gpsimd", [d_xde],
                        lambda e, ci=ci: e.dma_start(
                            out=xag_in[ci], in_=xdbl[0:96, :]),
                        sem_="s_xd", n=16)
            em.release("xdbl", d_xsp)
            d_ag = eop("gpsimd", [d_xsp],
                       lambda e, ci=ci: e.collective_compute(
                           "AllGather", ALU.bypass,
                           ins=[xag_in[ci]], outs=[xag_out[ci]],
                           replica_groups=[[0, 1], [2, 3], [4, 5], [6, 7]]),
                       sem_="s_ag", n=1)
            ag_dep[ci] = d_ag
            yield "m1"

          def part_b(ci):
            d_xl = eop("sync", [ag_dep[ci], em.need("xd2")],
                       lambda e, ci=ci: e.dma_start(
                           out=xd2[0:96, :, :],
                           in_=xag_out[ci].rearrange("r p t -> p r t")),
                       sem_="s_xl", n=16)
            d_xsum = eop("vector", [d_xl, em.need("xdbl_f")],
                         lambda e: e.tensor_add(xdbl_f[0:96, :],
                                                xd2[0:96, 0, :],
                                                xd2[0:96, 1, :]))
            em.release("xd2", d_xsum)
            d_xbf = eop("vector", [d_xsum, em.need("xdbl_bf")],
                        lambda e: e.tensor_copy(xdbl_bf[0:96, :],
                                                xdbl_f[0:96, :]))
            em.release("xdbl_f", d_xbf)
            d_xfsp = eop("gpsimd", [d_xbf],
                         lambda e, ci=ci: e.dma_start(
                             out=xdblb_dram[ci, :, :],
                             in_=xdbl_bf[64:96, :]),
                         sem_="s_xd2", n=16)
            yield "m1"

            d_dtmm = None
            for m in range(NDT):
                d_dtmm = eop("tensor",
                             [d_wdt if ci == 0 and m == 0 else None,
                              d_xbf if m == 0 else None,
                              em.need(("pDT", m % 2))],
                             lambda e, m=m: e.matmul(
                                 (pC1 if m % 2 == 0 else pC2)[:, 0:TC],
                                 wt_dt[0:64, m * 128:(m + 1) * 128],
                                 xdbl_bf[0:64, :], start=True, stop=True))

                d_abs = eop("scalar", [d_dtmm, em.need("sp_na")],
                            lambda e, m=m: e.activation(
                                sp_na[:],
                                (pC1 if m % 2 == 0 else pC2)[:, 0:TC],
                                AF.Abs, bias=vec_bdt[:, m:m + 1]))
                d_rel = eop("scalar", [d_dtmm, em.need(("sp_m", m % 2))],
                            lambda e, m=m: e.activation(
                                sp_m[m % 2][:],
                                (pC1 if m % 2 == 0 else pC2)[:, 0:TC],
                                AF.Relu, bias=vec_bdt[:, m:m + 1]))
                em.release(("pDT", m % 2), d_rel)
                d_exp = eop("scalar", [d_abs, em.need("sp_e")],
                            lambda e: e.activation(sp_e[:], sp_na[:], AF.Exp,
                                                   scale=-1.0))
                em.release("sp_na", d_exp)
                d_lg = eop("scalar", [d_exp, em.need(("sp_l", m % 2))],
                           lambda e, m=m: e.activation(sp_l[m % 2][:], sp_e[:],
                                                       AF.Ln, bias=1.0))
                em.release("sp_e", d_lg)

                d_dt = eop("vector", [d_rel, d_lg, em.need(("dt_t", m % 4))],
                           lambda e, m=m: e.tensor_add(dt_t[m % 4][:],
                                                       sp_m[m % 2][:],
                                                       sp_l[m % 2][:]))
                em.release(("sp_m", m % 2), d_dt)
                em.release(("sp_l", m % 2), d_dt)
                d_spd = eop("gpsimd", [d_dt],
                            lambda e, m=m, ci=ci: e.dma_start(
                                out=dt_dram[m, :, ci * TC:(ci + 1) * TC],
                                in_=dt_t[m % 4][:]),
                            sem_="s_dts%d" % (m % 4), n=16)
                em.release(("dt_t", m % 4), d_spd)
                if m % 4 == 3:
                    yield "m1"
            em.release("xdbl_bf", [d_dtmm, d_xfsp])

          ag_dep = {}
          for ci in range(NCH if PH >= 2 else 0):
            yield from part_a(ci)
            if ci > 0:
                yield from part_b(ci - 1)
          if PH >= 2:
            yield from part_b(NCH - 1)

        g_m1 = m1_stream()
        g_cast = ffn_cast_stream()
        for _ in g_m1:
            next(g_cast, None)
            next(g_cast, None)
        for _ in g_cast:
            pass
        ffn_cast_done = [("s_wc2b0", em.val("s_wc2b0")),
                         ("s_wc2b1", em.val("s_wc2b1"))]

        m1_spill_sems = (["s_xs%d" % m for m in range(NDT)] +
                         ["s_z%d" % i for i in range(4)] +
                         ["s_xd", "s_xd2"] +
                         ["s_dts%d" % i for i in range(4)])
        m1_sp_done = [(n, em.val(n)) for n in m1_spill_sems if em.val(n) > 0]
        ar.reset(0)
        em.phase_barrier()
        m2_dt = ar.take([NDT, TC])
        m2_dtx = ar.take([NDT, TC])
        y_acc = ar.take([NDT, TC])
        abar = [ar.take([TC]), ar.take([TC])]
        u_t = [ar.take([TC]), ar.take([TC])]
        h_t = [ar.take([TC]), ar.take([TC])]
        p_t = [ar.take([TC])]
        mixp = [ar.take([TC])]
        xs_g = [ar.take([TC])]
        bc_row = ar.take([2, TC], BF16).rearrange(
            "p a t -> p (a t)").rearrange(
            "p (o a t) -> p o a t", o=1, a=2)
        z_g = [ar.take([TC], BF16)]
        g_all = ar.take([NDT, TC], BF16)
        wt_o = [ar.take([NDT * 128], BF16), ar.take([NDT * 128], BF16)]
        x2 = ar.take([NKD, TC])
        xto = [ar.take([TC]), ar.take([TC])]
        u1t = [ar.take([TC])]
        u2t = [ar.take([TC])]
        rows2 = ar.take([3, TC])
        ln2 = ar.take([NKD, TC], BF16)
        hbuf = ar.take([NMH, TC], BF16)
        h3buf = ar.take([NMH, TC], BF16)
        wt_h = [ar.take([DH], BF16), ar.take([DH], BF16)]
        wt_s = [ar.take([NKD * 128], BF16), ar.take([NKD * 128], BF16)]
        xbf2 = [ar.take([TC], BF16)]
        xsq2 = [ar.take([TC], BF16)]
        assert ar.mark() <= ARENA_ELEMS, ar.mark()

        y_dep = {}
        st_dep = {}
        rs_deps = []
        m2_pre = {}

        def emit_m2_loads(ci, first):
            HD = NDT // 2
            d_a = edma("sync",
                       [m1_sp_done if first else None,
                        em.need(("m2_dt", 0)), em.need(("m2_dtx", 0))],
                       lambda e, ci=ci: [
                           e.dma_start(out=m2_dt[:, 0:HD, :],
                                       in_=dt_dram[0:HD, :,
                                                   ci * TC:(ci + 1) * TC]
                                       .rearrange("m p t -> p m t")),
                           e.dma_start(out=m2_dtx[:, 0:HD, :],
                                       in_=xs_dram[0:HD, :,
                                                   ci * TC:(ci + 1) * TC]
                                       .rearrange("m p t -> p m t"))],
                       "s_m2", 2)
            d_b = edma("sync",
                       [m1_sp_done if first else None,
                        em.need(("m2_dt", 1)), em.need(("m2_dtx", 1))],
                       lambda e, ci=ci: [
                           e.dma_start(out=m2_dt[:, HD:NDT, :],
                                       in_=dt_dram[HD:NDT, :,
                                                   ci * TC:(ci + 1) * TC]
                                       .rearrange("m p t -> p m t")),
                           e.dma_start(out=m2_dtx[:, HD:NDT, :],
                                       in_=xs_dram[HD:NDT, :,
                                                   ci * TC:(ci + 1) * TC]
                                       .rearrange("m p t -> p m t"))],
                       "s_m2", 2)
            return d_a, d_b

        def m2_chunk_stream(ci, first):
            g_deps = []
            HD = NDT // 2
            if ci in m2_pre:
                d_m2a, d_m2b = m2_pre.pop(ci)
            else:
                d_m2a, d_m2b = emit_m2_loads(ci, first)

            dtx_dep = {}
            for m in range(NDT):
                dtx_dep[m] = eop("vector",
                                 [d_m2a if m < HD else d_m2b],
                                 lambda e, m=m: e.tensor_mul(
                                     m2_dtx[:, m, :], m2_dt[:, m, :],
                                     m2_dtx[:, m, :]))

            def d_m2_for(m):
                return d_m2a if m < HD else d_m2b

            bc_dep = {}

            def emit_bc(s):
                sb = s % 2
                d_bcl = edma("sync", [em.need("bc_row")],
                             lambda e, ci=ci, s=s: [
                                 e.dma_start(out=bc_row[0:1, 0, 0, :],
                                             in_=xdblb_dram[ci,
                                                            s:s + 1, :]),
                                 e.dma_start(out=bc_row[0:1, 0, 1, :],
                                             in_=xdblb_dram[ci,
                                                            16 + s:17 + s, :])],
                             "s_bcl", 2)
                d_bc = egroup("tensor", [d_bcl, em.need(("pBC", sb))],
                              lambda e, s=s, sb=sb: (
                                  e.matmul((pC1 if sb == 0 else pC3)[:, 0:TC],
                                           ones_bf[:], bc_row[0:1, 0, 0, :],
                                           start=True, stop=True),
                                  e.matmul((pC2 if sb == 0 else pC4)[:, 0:TC],
                                           ones_bf[:], bc_row[0:1, 0, 1, :],
                                           start=True, stop=True)),
                              tag_last=2)
                em.release("bc_row", d_bc)
                bc_dep[s] = d_bc

            emit_bc(0)
            for s in range(DS):
                sb = s % 2
                if s + 1 < DS:
                    emit_bc(s + 1)
                d_bc = bc_dep[s]
                d_ab = {}
                d_u = {}
                d_sc = {}
                d_ym = {}
                last_dv = None
                for mm in range(NDT + 2):
                    if mm < NDT:
                        m, ab = mm, (s * NDT + mm) % 2
                        d_ab[m] = eop(
                            "scalar",
                            [d_m2_for(m) if s == 0 else None,
                             d_acols if (ci == 0 and s == 0 and m == 0)
                             else None,
                             em.need(("abar", ab))],
                            lambda e, s=s, m=m, ab=ab: e.activation(
                                abar[ab][:], m2_dt[:, m, :], AF.Exp,
                                scale=a_cols[:, s:s + 1]))
                    if 1 <= mm <= NDT:
                        m, ub, ab = mm - 1, (mm - 1) % 2, (s * NDT + mm - 1) % 2
                        d_sc[m] = eop(
                            "vector",
                            [d_ab[m], d_u[m], st_dep.get((m, s)),
                             em.need(("h_t", ub))],
                            lambda e, s=s, m=m, ub=ub, ab=ab:
                            e.tensor_tensor_scan(
                                h_t[ub][:], abar[ab][:], u_t[ub][:],
                                state[:, m, s:s + 1], ALU.mult, ALU.add))
                        em.release(("abar", ab), d_sc[m])
                        em.release(("u_t", ub), d_sc[m])
                    if mm < NDT:
                        m, ub = mm, mm % 2
                        d_u[m] = eop(
                            "vector",
                            [d_bc, dtx_dep[m] if s == 0 else None,
                             em.need(("u_t", ub))],
                            lambda e, s=s, m=m, ub=ub: e.tensor_mul(
                                u_t[ub][:], m2_dtx[:, m, :],
                                (pC1 if s % 2 == 0 else pC3)[:, 0:TC]))
                    if mm >= 2:
                        m, ub = mm - 2, (mm - 2) % 2
                        if s == 0:
                            d_y = eop("vector", [d_sc[m], y_dep.get(m)],
                                      lambda e, s=s, m=m, ub=ub: e.tensor_mul(
                                          y_acc[:, m, :], h_t[ub][:],
                                          (pC2 if s % 2 == 0
                                           else pC4)[:, 0:TC]))
                            d_st = eop("vector", [d_sc[m]],
                                       lambda e, s=s, m=m, ub=ub:
                                       e.tensor_copy(state[:, m, s:s + 1],
                                                     h_t[ub][:, TC - 1:TC]))
                            em.release(("h_t", ub), [d_y, d_st])
                        else:
                            d_ym[m] = eop("vector",
                                          [d_sc[m], em.need("p_t")],
                                          lambda e, s=s, m=m, ub=ub:
                                          e.tensor_mul(
                                              p_t[0][:], h_t[ub][:],
                                              (pC2 if s % 2 == 0
                                               else pC4)[:, 0:TC]))
                            d_st = eop("vector", [d_sc[m]],
                                       lambda e, s=s, m=m, ub=ub:
                                       e.tensor_copy(state[:, m, s:s + 1],
                                                     h_t[ub][:, TC - 1:TC]))
                            em.release(("h_t", ub), [d_ym[m], d_st])
                            d_y = eop("vector", [d_ym[m], y_dep.get(m)],
                                      lambda e, m=m: e.tensor_add(
                                          y_acc[:, m, :], y_acc[:, m, :],
                                          p_t[0][:]))
                            em.release("p_t", d_y)
                        st_dep[(m, s)] = d_st
                        y_dep[m] = d_y
                        last_dv = d_y
                    if mm % 3 == 2:
                        yield "s"
                em.release(("pBC", sb), last_dv)
            em.release(("m2_dt", 0), d_ab[HD - 1])
            em.release(("m2_dt", 1), d_ab[NDT - 1])
            em.release(("m2_dtx", 0), d_u[HD - 1])
            em.release(("m2_dtx", 1), d_u[NDT - 1])
            if ci + 1 < NCH:
                m2_pre[ci + 1] = emit_m2_loads(ci + 1, False)

            for m in range(NDT):
                d_gl = edma("sync",
                            [em.need("xs_g"), em.need("z_g")],
                            lambda e, ci=ci, m=m: [
                                e.dma_start(out=xs_g[0][:],
                                            in_=xs_dram[m, :,
                                                        ci * TC:(ci + 1) * TC]),
                                e.dma_start(out=z_g[0][:],
                                            in_=z_dram[m, :,
                                                       ci * TC:(ci + 1) * TC])],
                            "s_gt", 2)
                d_sz = eop("scalar", [d_gl, em.need("mixp")],
                           lambda e, m=m: e.activation(
                               mixp[0][:], z_g[0][:], AF.Silu))
                em.release("z_g", d_sz)
                d_t1 = eop("vector",
                           [d_gl, y_dep[m], em.need("xs_g")],
                           lambda e, m=m: e.scalar_tensor_tensor(
                               xs_g[0][:], xs_g[0][:], vec_d[:, m:m + 1],
                               y_acc[:, m, :], ALU.mult, ALU.add))
                y_dep[m] = d_t1
                d_g = eop("vector", [d_t1, d_sz, em.need(("g_all", m))],
                          lambda e, m=m: e.tensor_mul(
                              g_all[:, m, :], xs_g[0][:], mixp[0][:]))
                em.release("mixp", d_g)
                em.release("xs_g", d_g)
                g_deps.append(d_g)
                if m % 4 == 3:
                    yield "gate"

            d_mo = None
            pend = None

            def wout_evac(mt, d_mm_):
                d_me = eop("scalar", [d_mm_, em.need("mixp")],
                           lambda e, mt=mt: e.copy(
                               mixp[0][:],
                               (pC3 if mt % 2 == 0 else pC4)[:, 0:TC]))
                em.release(("pwo", mt % 2), d_me)
                wout_evac.last = d_me
                d_ms = edma("gpsimd", [d_me],
                            lambda e, ci=ci, mt=mt: [
                                e.dma_start(
                                    out=rs_in[ci, 0, mt].rearrange(
                                        "p (g w) -> p g w", w=64),
                                    in_=mixp[0].rearrange(
                                        "p (g two w) -> p g two w",
                                        two=2, w=64)[:, :, 0, :]),
                                e.dma_start(
                                    out=rs_in[ci, 1, mt].rearrange(
                                        "p (g w) -> p g w", w=64),
                                    in_=mixp[0].rearrange(
                                        "p (g two w) -> p g two w",
                                        two=2, w=64)[:, :, 1, :])],
                            "s_mx", 2)
                em.release("mixp", d_ms)

            for mt in range(NKD):
                d_wo = eop("sync", [em.need(("wt_o", mt % 2)), m0_done],
                           lambda e, mt=mt: e.dma_start(
                               out=wt_o[mt % 2][:],
                               in_=wb_out[mt].rearrange("p a b -> p (a b)")),
                           sem_="s_wo%d" % (mt % 2), n=16)
                d_mo = egroup("tensor",
                              [d_wo, em.need(("pwo", mt % 2)),
                               em.need(("pBC", 1)) if mt < 2 else None] +
                              (g_deps if mt == 0 else []),
                              lambda e, mt=mt: [
                                  e.matmul(
                                      (pC3 if mt % 2 == 0 else pC4)[:, 0:TC],
                                      wt_o[mt % 2][:, k * 128:(k + 1) * 128],
                                      g_all[:, k, :], start=(k == 0),
                                      stop=(k == NDT - 1))
                                  for k in range(NDT)])
                em.release(("wt_o", mt % 2), d_mo)
                if pend is not None:
                    wout_evac(*pend)
                pend = (mt, d_mo)
                if mt % 4 == 3:
                    yield "wout"
            wout_evac(*pend)
            em.release(("pBC", 1), wout_evac.last)
            for m in range(NDT):
                em.release(("g_all", m), d_mo)
            d_rs = eop("gpsimd", [("s_mx", em.val("s_mx"))],
                       lambda e, ci=ci: e.collective_compute(
                           "ReduceScatter", ALU.add,
                           ins=[rs_in[ci]], outs=[rs_out[ci]],
                           replica_groups=[[0, 1], [2, 3], [4, 5], [6, 7]]),
                       sem_="s_rs", n=1)
            rs_deps.append(d_rs)
            yield "rs"

        def ffn_block_stream(fb):
            fsl = slice(fb * TC, (fb + 1) * TC)
            d_mg = edma("sync",
                        [rs_deps[2 * fb + 1], em.need("x2w")],
                        lambda e, fb=fb: [
                            e.dma_start(out=x2[:, :, 0:TC // 2],
                                        in_=rs_out[2 * fb].rearrange(
                                            "m p t -> p m t")),
                            e.dma_start(out=x2[:, :, TC // 2:TC],
                                        in_=rs_out[2 * fb + 1].rearrange(
                                            "m p t -> p m t"))],
                        "s_ff", 2)

            x2_deps = []
            for m in range(NKD):
                xb = m % 2
                d_xr = eop("sync", [em.need(("xto", xb))],
                           lambda e, m=m, fsl=fsl, xb=xb: e.dma_start(
                               out=xto[xb][:],
                               in_=x_res[m * 128:(m + 1) * 128, fsl]),
                           sem_="s_xr%d" % xb, n=16)
                d_x2 = eop("gpsimd", [d_mg, d_xr, em.need(("x2", m))],
                           lambda e, m=m, xb=xb: e.tensor_add(
                               x2[:, m, :], x2[:, m, :], xto[xb][:]))
                em.release(("xto", xb), d_x2)
                x2_deps.append(d_x2)
            yield "x2"

            ln2_deps = emit_ln(x2, x2_deps, ln2, xbf2, xsq2, rows2,
                               xto, "ff", xto_key="xto")
            yield "ln2"

            d_umm_last = None
            h_deps = []
            upend = None

            def u_evac(m, d1, d2):
                b1 = pA if m % 2 == 0 else pB
                b2 = pS1 if m % 2 == 0 else pS2
                d_u1 = eop("scalar", [d1, em.need("u1t")],
                           lambda e, m=m, b1=b1: e.activation(
                               u1t[0][:], b1[:, 0:TC], AF.Silu,
                               scale=vw1s[:, m:m + 1]))
                em.release(("p4", m % 2), d_u1)
                d_u2 = eop("scalar", [d2, em.need("u2t")],
                           lambda e, m=m, b2=b2: e.activation(
                               u2t[0][:], b2[:, 0:TC],
                               AF.Identity, scale=vw2s[:, m:m + 1]))
                em.release(("p4", 2 + m % 2), d_u2)
                d_h = eop("gpsimd", [d_u1, d_u2, em.need(("hbuf", m))],
                          lambda e, m=m: e.tensor_mul(hbuf[:, m, :],
                                                      u1t[0][:], u2t[0][:]))
                em.release("u1t", d_h)
                em.release("u2t", d_h)
                h_deps.append(d_h)

            for m in range(NMH):
                b1 = pA if m % 2 == 0 else pB
                b2 = pS1 if m % 2 == 0 else pS2
                d_w1 = eop("sync", [em.need(("wt_s", 0)), ffn_cast_done],
                           lambda e, m=m: e.dma_start(
                               out=wt_s[0][:],
                               in_=wb_1[m].rearrange("p a b -> p (a b)")),
                           sem_="s_ws0", n=16)
                d_mm1 = egroup("tensor",
                               [d_w1, em.need(("p4", m % 2))] +
                               (ln2_deps if m == 0 else []),
                               lambda e, m=m, b1=b1: [
                                   e.matmul(b1[:, 0:TC],
                                            wt_s[0][:, k * 128:(k + 1) * 128],
                                            ln2[:, k, :], start=(k == 0),
                                            stop=(k == NKD - 1))
                                   for k in range(NKD)])
                em.release(("wt_s", 0), d_mm1)
                d_w2 = eop("sync", [em.need(("wt_s", 1)), ffn_cast_done],
                           lambda e, m=m: e.dma_start(
                               out=wt_s[1][:],
                               in_=wb_2[m].rearrange("p a b -> p (a b)")),
                           sem_="s_ws1", n=16)
                d_mm2 = egroup("tensor",
                               [d_w2, em.need(("p4", 2 + m % 2))] +
                               (ln2_deps if m == 0 else []),
                               lambda e, m=m, b2=b2: [
                                   e.matmul(b2[:, 0:TC],
                                            wt_s[1][:, k * 128:(k + 1) * 128],
                                            ln2[:, k, :], start=(k == 0),
                                            stop=(k == NKD - 1))
                                   for k in range(NKD)])
                em.release(("wt_s", 1), d_mm2)
                d_umm_last = d_mm2
                if upend is not None:
                    u_evac(*upend)
                upend = (m, d_mm1, d_mm2)
                yield "u"
            u_evac(*upend)
            for k in range(NKD):
                em.release(("ffln", k), d_umm_last)

            def big_layer(wb_src, nmt, rhs_get, rhs_deps, evac, first_dep):
                d_mm = None
                outs = []
                for m in range(nmt):
                    wbuf = m % 2
                    d_w = eop("sync", [em.need(("wt_h", wbuf)), ffn_cast_done],
                              lambda e, wb_src=wb_src, m=m, wbuf=wbuf:
                              e.dma_start(out=wt_h[wbuf][:],
                                          in_=wb_src[m].rearrange(
                                              "p a b -> p (a b)")),
                              sem_="s_wh%d" % wbuf, n=16)
                    d_mm = egroup("tensor",
                                  [d_w, first_dep if m == 0 else None,
                                   em.need(("pAB", m % 2))] +
                                  (rhs_deps if m == 0 else []),
                                  lambda e, m=m, wbuf=wbuf: [
                                      e.matmul(
                                          (pA if m % 2 == 0 else pB)[:, 0:TC],
                                          wt_h[wbuf][:, k * 128:(k + 1) * 128],
                                          rhs_get(k), start=(k == 0),
                                          stop=(k == NMH - 1))
                                      for k in range(NMH)])
                    em.release(("wt_h", wbuf), d_mm)
                    outs.append(evac(m, d_mm))
                return outs, d_mm

            def evac_h3(m, d_mm):
                d_e = eop("scalar", [d_mm, em.need(("h3buf", m))],
                          lambda e, m=m: e.activation(
                              h3buf[:, m, :],
                              (pA if m % 2 == 0 else pB)[:, 0:TC], AF.Gelu,
                              scale=vh1s[:, m:m + 1]))
                em.release(("p4", m % 2), d_e)
                return d_e

            def evac_h4(m, d_mm):
                d_e = eop("scalar", [d_mm, em.need(("hbuf", m))],
                          lambda e, m=m: e.activation(
                              hbuf[:, m, :],
                              (pA if m % 2 == 0 else pB)[:, 0:TC], AF.Gelu,
                              scale=vh2s[:, m:m + 1]))
                em.release(("p4", m % 2), d_e)
                return d_e

            def evac_out(m, d_mm):
                d_e = eop("scalar", [d_mm, em.need("u1t")],
                          lambda e, m=m: e.activation(
                              u1t[0][:],
                              (pA if m % 2 == 0 else pB)[:, 0:TC],
                              AF.Identity, scale=vw3s[:, m:m + 1]))
                em.release(("p4", m % 2), d_e)
                d_o = eop("gpsimd", [d_e, x2_deps[m], em.need(("xto", m % 2))],
                          lambda e, m=m: e.tensor_add(xto[m % 2][:],
                                                      u1t[0][:],
                                                      x2[:, m, :]))
                em.release("u1t", d_o)
                em.release(("x2", m), d_o)
                x2_read_deps.append(d_o)
                d_os = eop("gpsimd", [d_o],
                           lambda e, m=m, fb=fb: e.dma_start(
                               out=out_fm[m * 128:(m + 1) * 128,
                                          fb * TC:(fb + 1) * TC],
                               in_=xto[m % 2][:]),
                           sem_="s_o%d" % (m % 2), n=16)
                em.release(("xto", m % 2), d_os)
                return d_os

            x2_read_deps = []
            h3_deps = []
            d_h1_last = None
            hpend = None
            for m in range(NMH):
                wbuf = m % 2
                d_w = eop("sync", [em.need(("wt_h", wbuf)), ffn_cast_done],
                          lambda e, m=m, wbuf=wbuf: e.dma_start(
                              out=wt_h[wbuf][:],
                              in_=wb_h1[m].rearrange("p a b -> p (a b)")),
                          sem_="s_wh%d" % wbuf, n=16)
                d_mm = egroup("tensor",
                              [d_w, em.need(("p4", m % 2))] +
                              (h_deps if m == 0 else []),
                              lambda e, m=m, wbuf=wbuf: [
                                  e.matmul((pA if m % 2 == 0 else pB)[:, 0:TC],
                                           wt_h[wbuf][:, k * 128:(k + 1) * 128],
                                           hbuf[:, k, :], start=(k == 0),
                                           stop=(k == NMH - 1))
                                  for k in range(NMH)])
                em.release(("wt_h", wbuf), d_mm)
                d_h1_last = d_mm
                if hpend is not None:
                    h3_deps.append(evac_h3(*hpend))
                hpend = (m, d_mm)
                yield "h1"
            h3_deps.append(evac_h3(*hpend))
            for m in range(NMH):
                em.release(("hbuf", m), d_h1_last)

            h4_deps = []
            d_h2_last = None
            hpend = None
            for m in range(NMH):
                wbuf = m % 2
                d_w = eop("sync", [em.need(("wt_h", wbuf)), ffn_cast_done],
                          lambda e, m=m, wbuf=wbuf: e.dma_start(
                              out=wt_h[wbuf][:],
                              in_=wb_h2[m].rearrange("p a b -> p (a b)")),
                          sem_="s_wh%d" % wbuf, n=16)
                d_mm = egroup("tensor",
                              [d_w, em.need(("p4", m % 2))] +
                              (h3_deps if m == 0 else []),
                              lambda e, m=m, wbuf=wbuf: [
                                  e.matmul((pA if m % 2 == 0 else pB)[:, 0:TC],
                                           wt_h[wbuf][:, k * 128:(k + 1) * 128],
                                           h3buf[:, k, :], start=(k == 0),
                                           stop=(k == NMH - 1))
                                  for k in range(NMH)])
                em.release(("wt_h", wbuf), d_mm)
                d_h2_last = d_mm
                if hpend is not None:
                    h4_deps.append(evac_h4(*hpend))
                hpend = (m, d_mm)
                yield "h2"
            h4_deps.append(evac_h4(*hpend))
            for m in range(NMH):
                em.release(("h3buf", m), d_h2_last)

            d_w3_last = None
            hpend = None
            for m in range(NKD):
                wbuf = m % 2
                d_w = eop("sync", [em.need(("wt_h", wbuf)), ffn_cast_done],
                          lambda e, m=m, wbuf=wbuf: e.dma_start(
                              out=wt_h[wbuf][:],
                              in_=wb_3[m].rearrange("p a b -> p (a b)")),
                          sem_="s_wh%d" % wbuf, n=16)
                d_mm = egroup("tensor",
                              [d_w, em.need(("p4", m % 2))] +
                              (h4_deps if m == 0 else []),
                              lambda e, m=m, wbuf=wbuf: [
                                  e.matmul((pA if m % 2 == 0 else pB)[:, 0:TC],
                                           wt_h[wbuf][:, k * 128:(k + 1) * 128],
                                           hbuf[:, k, :], start=(k == 0),
                                           stop=(k == NMH - 1))
                                  for k in range(NMH)])
                em.release(("wt_h", wbuf), d_mm)
                d_w3_last = d_mm
                if hpend is not None:
                    evac_out(*hpend)
                hpend = (m, d_mm)
                yield "w3"
            evac_out(*hpend)
            em.release("x2w", x2_read_deps)
            for m in range(NMH):
                em.release(("hbuf", m), d_w3_last)

        NSC = NCH if PH >= 3 else 0
        NFB = NSC // 2 if PH >= 4 else 0
        FFN_UNITS_PER_YIELD = 1
        active_ffn = None

        def drive_ffn(k):
            nonlocal_done = False
            if active_ffn is None:
                return
            for _ in range(k):
                if next(active_ffn, "__done__") == "__done__":
                    break

        for fb in range((NSC + 1) // 2):
            for cidx in (2 * fb, 2 * fb + 1):
                if cidx >= NSC:
                    break
                for unit in m2_chunk_stream(cidx, first=(cidx == 0)):
                    drive_ffn(FFN_UNITS_PER_YIELD)
            if fb < NFB:
                if active_ffn is not None:
                    for _ in active_ffn:
                        pass
                active_ffn = ffn_block_stream(fb)
        if active_ffn is not None:
            for _ in active_ffn:
                pass

        final_waits = [(n, em.val(n)) for n in list(S.keys())
                       if n not in ENGINE_SEMS and em.val(n) > 0]

        @em.on("gpsimd")
        def _(e, final_waits=final_waits):
            for n, v in final_waits:
                e.wait_ge(S[n], v)

        @block.sync
        def _(eng):
            for fn in em.q["sync"]:
                fn(eng)

        @block.tensor
        def _(eng):
            for fn in em.q["tensor"]:
                fn(eng)

        @block.vector
        def _(eng):
            for fn in em.q["vector"]:
                fn(eng)

        @block.scalar
        def _(eng):
            for fn in em.q["scalar"]:
                fn(eng)

        @block.gpsimd
        def _(eng):
            for fn in em.q["gpsimd"]:
                fn(eng)

    return nc


def kernel(**inputs):
    ii = {k: np.asarray(v, dtype=np.float32) for k, v in inputs.items()}
    x = ii["L0_attn_x"]
    W_in, W_x = ii["L0_attn_W_in"], ii["L0_attn_W_x"]
    W_dt, W_out = ii["L0_attn_W_dt"], ii["L0_attn_W_out"]
    conv_w, conv_b = ii["L0_attn_conv_w"], ii["L0_attn_conv_b"]
    b_dt, Dv = ii["L0_attn_b_dt"], ii["L0_attn_D"]

    nc = build_program()

    shared = {
        "w_1": pack_w(ii["L0_ffn_w1_w"]),
        "w_2": pack_w(ii["L0_ffn_w2_w"]),
        "w_h1": pack_w(ii["L0_ffn_h1_w"]),
        "w_h2": pack_w(ii["L0_ffn_h2_w"]),
        "w_3": pack_w(ii["L0_ffn_w3_w"]),
        "a_log": ii["L0_attn_A_log"][None, :],
        "w1s_p": pack_vec(ii["L0_ffn_w1_s"]),
        "w2s_p": pack_vec(ii["L0_ffn_w2_s"]),
        "h1s_p": pack_vec(ii["L0_ffn_h1_s"]),
        "h2s_p": pack_vec(ii["L0_ffn_h2_s"]),
        "w3s_p": pack_vec(ii["L0_ffn_w3_s"]),
    }

    in_maps = []
    for c in range(8):
        b, j = c // 2, c % 2
        ch = slice(j * CH, (j + 1) * CH)
        m = dict(shared)
        xt = np.ascontiguousarray(x[b].T)
        m["x_fm"] = xt
        m["x_res"] = np.ascontiguousarray(
            xt.reshape(DM, L // 64, 64)[:, j::2, :].reshape(DM, L // 2))
        m["w_in"] = pack_w(np.ascontiguousarray(
            np.concatenate([W_in[ch], W_in[2048 + j * CH:2048 + (j + 1) * CH]],
                           axis=0)))
        m["w_x"] = pack_w(np.ascontiguousarray(W_x[:, ch]))
        m["w_dtp"] = pack_w(np.ascontiguousarray(W_dt[ch]))
        m["w_out"] = pack_w(np.ascontiguousarray(W_out[:, ch]))
        m["conv_w_p"] = np.ascontiguousarray(
            conv_w[ch].reshape(NDT, 128, DCONV).transpose(1, 0, 2))
        m["conv_b_p"] = pack_vec(conv_b[ch])
        m["b_dt_p"] = pack_vec(b_dt[ch])
        m["d_p"] = pack_vec(Dv[ch])
        in_maps.append(m)

    res = run_bass_kernel_spmd(nc, in_maps, core_ids=list(range(8)))
    out = np.empty((B, L, DM), dtype=np.float32)
    for b in range(B):
        ofm = np.empty((DM, L), dtype=np.float32)
        v = ofm.reshape(DM, L // 64, 64)
        v[:, 0::2, :] = res.results[2 * b]["out_fm"].reshape(DM, L // 128, 64)
        v[:, 1::2, :] = res.results[2 * b + 1]["out_fm"].reshape(DM, L // 128, 64)
        out[b] = ofm.T
    return out
